# version 19
# baseline (speedup 1.0000x reference)
import contextlib
from collections import defaultdict

import numpy as np
import ml_dtypes
import concourse.bass as bass
import concourse.mybir as mybir
from concourse.bass_utils import run_bass_kernel_spmd

F32 = mybir.dt.float32
BF16 = mybir.dt.float16
AF = mybir.ActivationFunctionType
ALU = mybir.AluOpType
AX = mybir.AxisListType

ENGINES = ("sync", "vector", "tensor", "scalar", "gpsimd")


class Cfg:
    def __init__(self, B=2, S=8192, D=2048, H=16, HF=5632, ncores=8, dbg=False):
        self.B, self.S, self.D, self.H, self.HF = B, S, D, H, HF
        self.ncores = ncores
        self.R = ncores // B
        self.T = B * S // ncores
        self.NT = self.T // 128
        self.C = D // 128
        self.E = D // 128
        self.HT = HF // 128
        self.KT = S // 128
        self.DH = D // H
        assert self.DH == 128
        self.QCW = min(512, self.T)
        self.NQC = self.T // self.QCW
        self.G = 2
        assert self.KT % self.G == 0
        self.NG = self.KT // self.G
        self.SCW = min(512, self.T)
        self.NSC = self.T // self.SCW
        self.SCALE = 1.0 / float(np.sqrt(np.float64(self.DH)))
        self.dbg = dbg


class Walk:

    def __init__(self, name, eng, sems):
        self.n = name
        self.e = eng
        self.h = sems
        self.c = defaultdict(int)

    def ins(self, eng, f, inc=None, by=1):
        if self.n == eng:
            r = f(self.e)
            if inc is not None:
                r.then_inc(self.h[inc], by)
        if inc is not None:
            self.c[inc] += by

    def wait(self, eng, sem, val):
        if val is None or val <= 0:
            return
        if self.n == eng:
            self.e.wait_ge(self.h[sem], val)

    def snap(self, sem):
        return self.c[sem]

    def barrier(self):
        for eng in ENGINES:
            self.ins(eng, lambda e: e.nop(), inc="bar")
        v = self.c["bar"]
        for eng in ENGINES:
            self.wait(eng, "bar", v)
            for s_ in ("stc", "sth0", "sth1"):
                self.wait(eng, s_, self.c[s_])


class Builder:
    def __init__(self, cfg):
        self.cfg = cfg
        nc = bass.Bass("TRN2", target_bir_lowering=False)
        self.nc = nc
        c = cfg

        dt = nc.dram_tensor
        self.x = dt("x", [c.T, c.D], F32, kind="ExternalInput")
        self.wq = dt("wq", [c.D, c.D], F32, kind="ExternalInput")
        self.wk = dt("wk", [c.D, c.D], F32, kind="ExternalInput")
        self.wv = dt("wv", [c.D, c.D], F32, kind="ExternalInput")
        self.wo = dt("wo", [c.D, c.D], F32, kind="ExternalInput")
        self.w1 = dt("w1", [c.HF, c.D], F32, kind="ExternalInput")
        self.w2 = dt("w2", [c.HF, c.D], F32, kind="ExternalInput")
        self.w3 = dt("w3", [c.D, c.HF], F32, kind="ExternalInput")
        self.ident = dt("ident", [128, 128], BF16, kind="ExternalInput")
        self.consts = dt("consts", [128, 2], F32, kind="ExternalInput")
        self.ones = dt("ones", [128, 128], F32, kind="ExternalInput")
        self.out = dt("out", [c.T, c.D], F32, kind="ExternalOutput")

        self.cc_in_k = dt("cc_in_k", [c.D, c.T], BF16)
        self.cc_out_k = dt("cc_out_k", [c.R * c.D, c.T], BF16)
        self.cc_in_v = dt("cc_in_v", [c.T, c.D], BF16)
        self.cc_out_v = dt("cc_out_v", [c.R * c.T, c.D], BF16)
        self.x1_dram = dt("x1_dram", [128, c.NT * c.D], BF16)
        self.xn2_dram = dt("xn2_dram", [128, c.NT * c.D], BF16)
        self.hid_dram = dt("hid_dram", [c.NSC * 128, c.HT * c.SCW], BF16)
        self.q3t_dram = dt("q3t_dram", [c.E * 128, c.HT * 128], BF16)
        self.q12_dram = [dt(f"q12_dram{i}", [128, c.D], BF16) for i in range(2)]
        if c.dbg:
            self.dbg_xnT = dt("dbg_xnT", [128, c.C * c.T], BF16, kind="ExternalOutput")
            self.dbg_qT = dt("dbg_qT", [128, c.C * c.T], BF16, kind="ExternalOutput")
            self.dbg_att = dt("dbg_att", [128, c.C * c.T], BF16, kind="ExternalOutput")
            self.dbg_thr = dt("dbg_thr", [128, 3], F32, kind="ExternalOutput")

        st = nc.sbuf_tensor
        self.root = contextlib.ExitStack()
        r = self.root.enter_context
        self.id_b = r(st("id_b", [128, 128], BF16))
        self.ones_b = r(st("ones_b", [128, 128], BF16))
        self.cst = r(st("cst", [128, 2], F32))
        self.ones_f = r(st("ones_f", [128, 128], F32))
        self.xstrip = r(st("xstrip", [128, 2, c.D], BF16))
        self.sgnb = r(st("sgnb", [128, c.D], BF16))
        self.nb = r(st("nb", [128, 2, c.D], BF16))
        self.sqb = r(st("sqb", [128, 1024], BF16))
        self.smalls = r(st("smalls", [128, 34], F32))
        sm = self.smalls
        self.sx = sm[:, 0:1]
        self.ssq = sm[:, 1:2]
        self.mn = sm[:, 2:3]
        self.m2 = sm[:, 3:4]
        self.var = sm[:, 4:5]
        self.std = sm[:, 5:6]
        self.rstd = sm[:, 6:7]
        self.gtmp = sm[:, 7:8]
        self.gacc = [sm[:, 8 + i:9 + i] for i in range(3)]
        self.thr = [sm[:, 11 + i:12 + i] for i in range(3)]
        self.nthr = [sm[:, 14 + i:15 + i] for i in range(3)]
        self.gtmpV = sm[:, 17:18]
        self.gaccV = [sm[:, 18 + i:19 + i] for i in range(3)]
        self.lns = []
        for b_ in range(2):
            o_ = 21 + 6 * b_
            self.lns.append(dict(sx=sm[:, o_:o_ + 1], ssq=sm[:, o_ + 1:o_ + 2],
                                 mn=sm[:, o_ + 2:o_ + 3], m2=sm[:, o_ + 3:o_ + 4],
                                 var=sm[:, o_ + 4:o_ + 5], rstd=sm[:, o_ + 5:o_ + 6]))

        self.scope_qt = contextlib.ExitStack()
        self.QT = self.scope_qt.enter_context(st("QT", [128, c.C, c.T], BF16))

        self.scope_a = contextlib.ExitStack()
        self.xnT = self.scope_a.enter_context(st("xnT", [128, c.C, c.T], BF16))
        self.wTv = self.scope_a.enter_context(st("wTv", [128, c.C, 512], BF16))
        self.eb = self.scope_a.enter_context(st("eb", [128, 2, max(c.T, c.D // 2)], BF16))
        self.wstripA = self.scope_a.enter_context(st("wstripA", [128, 2, c.D], F32))
        self.wbfA = self.scope_a.enter_context(st("wbfA", [128, 2, c.D], BF16))
        self.wTA = self.scope_a.enter_context(st("wTA", [128, 2, c.C, 128], BF16))

        self.ps_a = contextlib.ExitStack()
        pt = nc.psum_tensor
        self.tbA = [self.ps_a.enter_context(pt(f"tbA{i}", [128, 1024], BF16)) for i in range(2)]
        self.pkA = [self.ps_a.enter_context(pt(f"pkA{i}", [128, 512], F32)) for i in range(2)]

    def alloc_rest(self):
        nc, c = self.nc, self.cfg
        st, pt = nc.sbuf_tensor, nc.psum_tensor
        self.scope_a.close()
        self.ps_a.close()
        self.scope_b = contextlib.ExitStack()
        r = self.scope_b.enter_context
        self.KH = r(st("KH", [128, 2, c.R * c.T], BF16))
        self.VH = r(st("VH", [128, 2, c.KT, 128], BF16))
        self.gbuf = r(st("gbuf", [128, 6, 1024], F32))
        self.PT = r(st("PT", [128, 6, c.G * c.QCW], BF16))
        self.Zacc = [r(st(f"Zacc{p_}", [128, 2 * c.QCW], BF16)) for p_ in range(2)]
        self.rz = r(st("rz", [128, c.QCW], F32))
        self.rzh = r(st("rzh", [128, c.QCW], BF16))
        self.ps_b = contextlib.ExitStack()
        self.Sp = [self.ps_b.enter_context(pt(f"Sp{i}", [128, c.G * c.QCW], F32)) for i in range(2)]
        self.Op = [self.ps_b.enter_context(pt(f"Op{i}", [128, c.QCW], F32)) for i in range(2)]
        self.ZB = [self.ps_b.enter_context(pt(f"ZB{i}", [128, 512], F32)) for i in range(2)]
        self.scope_b.close()
        self.ps_b.close()
        self.scope_c = contextlib.ExitStack()
        self.wTo = self.scope_c.enter_context(st("wTo", [128, c.C, c.D], BF16))
        self.x1f = self.scope_c.enter_context(st("x1f", [128, c.D], F32))
        self.x1c = self.sgnb
        self.wstripC = self.scope_c.enter_context(st("wstripC", [128, 2, c.D], F32))
        self.wbfC = self.scope_c.enter_context(st("wbfC", [128, 2, c.D], BF16))
        self.wTC = self.scope_c.enter_context(st("wTC", [128, 2, c.C, 128], BF16))
        self.ps_c = contextlib.ExitStack()
        self.tbC = [self.ps_c.enter_context(pt(f"tbC{i}", [128, 1024], BF16)) for i in range(2)]
        self.poC = self.ps_c.enter_context(pt("poC", [128, c.D], F32))
        self.scope_c.close()
        self.scope_qt.close()
        self.ps_c.close()
        self.scope_e = contextlib.ExitStack()
        r = self.scope_e.enter_context
        self.xn2T = r(st("xn2T", [128, c.C, c.T], BF16))
        self.w3f = r(st("w3f", [128, c.HF], F32))
        self.q3s = r(st("q3s", [128, c.HF], BF16))
        self.q3stage = r(st("q3stage", [128, c.HT, 128], BF16))
        self.wstrip2 = r(st("wstrip2", [128, 2, c.D], F32))
        self.qbf = r(st("qbf", [128, 2, c.D], BF16))
        self.q1T = r(st("q1T", [128, 2, c.C, 128], BF16))
        self.q2T = r(st("q2T", [128, 2, c.C, 128], BF16))
        self.geb = r(st("geb", [128, c.HF], BF16))
        self.sig = r(st("sig", [128, 2, 512], F32))
        self.hidb = r(st("hidb", [128, 2, 512], BF16))
        self.ps_d = contextlib.ExitStack()
        p = self.ps_d.enter_context
        self.tbD = [p(pt(f"tbD{i}", [128, 1024], BF16)) for i in range(2)]
        self.pfD = [p(pt(f"pfD{i}", [128, 512], F32)) for i in range(4)]
        self.scope_e.close()
        self.ps_d.close()
        self.scope_f = contextlib.ExitStack()
        r = self.scope_f.enter_context
        self.hid = r(st("hid", [128, c.HT, c.SCW], BF16))
        self.q3rd = r(st("q3rd", [128, 2, c.HT, 128], BF16))
        self.outsb = r(st("outsb", [128, c.SCW // 128, c.D], F32))
        self.panel = r(st("panel", [128, 2, 512], BF16))
        self.ps_e = contextlib.ExitStack()
        p = self.ps_e.enter_context
        self.accE = [p(pt(f"accE{i}", [128, 512], F32)) for i in range(2)]
        self.tpzE = [p(pt(f"tpzE{i}", [128, 512], BF16)) for i in range(2)]

    def close_all(self):
        self.scope_f.close()
        self.ps_e.close()
        self.root.close()

    def _lp(self, f):
        def g(e):
            with self.nc.allow_low_precision(reason="fp16 softmax partial sums (error ~3e-4 rel, inside tolerance)"):
                return f(e)
        return g

    def _replica_groups(self):
        c = self.cfg
        return [list(range(b * c.R, (b + 1) * c.R)) for b in range(c.B)]

    def emit(self, w):
        self._tp_tags = {}
        self._pt_pv = None
        self._pt_z = None
        self.tb, self.pk = self.tbA, self.pkA
        self.wstrip, self.wbf, self.wT = self.wstripA, self.wbfA, self.wTA
        self.ph_ln1(w)
        self.ph_proj(w)
        w.barrier()
        self.ph_attn(w)
        w.barrier()
        self.tb = self.tbC
        self.wstrip, self.wbf, self.wT = self.wstripC, self.wbfC, self.wTC
        self.ph_outproj(w)
        w.barrier()
        self.tb = self.tbD
        self.ph_ffn1(w)
        w.barrier()
        self.ph_ffn2(w)

    def ln_tile(self, w, xin_f, out_bf, out_war=None, idx=0, disc=None, eps=1e-5):
        c = self.cfg
        D = c.D
        s = self.lns[idx]
        sqb = disc
        w.ins("scalar", lambda e, sqb=sqb: e.activation(out=sqb, in_=xin_f[:, 0:c.D // 2],
                                               func=AF.Square, accum_out=s["ssq"]), inc="S")
        w.wait("scalar", "S", w.snap("S"))
        w.ins("scalar", lambda e, sqb=sqb: e.activation(out=sqb, in_=xin_f[:, c.D // 2:c.D],
                                               func=AF.Square, accum_out=s["m2"]), inc="S")
        sq_done = w.snap("S")
        w.ins("vector", lambda e: e.tensor_reduce(out=s["sx"], in_=xin_f,
                                                  axis=AX.X, op=ALU.add), inc="V")
        w.wait("vector", "V", w.snap("V"))
        w.ins("vector", lambda e: e.tensor_scalar(out=s["mn"], in0=s["sx"],
                                                  scalar1=1.0 / D, scalar2=None, op0=ALU.mult), inc="V")
        w.wait("vector", "S", sq_done)
        w.wait("vector", "V", w.snap("V"))
        w.ins("vector", lambda e: e.tensor_tensor(out=s["ssq"], in0=s["ssq"],
                                                  in1=s["m2"], op=ALU.add), inc="V")
        w.wait("vector", "V", w.snap("V"))
        w.ins("vector", lambda e: e.tensor_tensor(out=s["m2"], in0=s["mn"],
                                                  in1=s["mn"], op=ALU.mult), inc="V")
        w.wait("vector", "V", w.snap("V"))
        w.ins("vector", lambda e: e.scalar_tensor_tensor(out=s["var"], in0=s["ssq"],
                                                         scalar=1.0 / D, in1=s["m2"],
                                                         op0=ALU.mult, op1=ALU.subtract), inc="V")
        v_var = w.snap("V")
        w.wait("scalar", "V", v_var)
        w.ins("scalar", lambda e: e.activation(out=s["var"], in_=s["var"],
                                               func=AF.Sqrt, bias=self.cst[:, 0:1]), inc="S")
        s_sqrt = w.snap("S")
        w.wait("vector", "S", s_sqrt)
        w.ins("vector", lambda e: e.reciprocal(out=s["rstd"], in_=s["var"]), inc="V")
        w.wait("vector", "V", w.snap("V"))
        if out_war is not None:
            w.wait("vector", out_war[0], out_war[1])
        w.ins("vector", lambda e: e.tensor_scalar(out=out_bf, in0=xin_f,
                                                  scalar1=s["mn"], scalar2=s["rstd"],
                                                  op0=ALU.subtract, op1=ALU.mult), inc="V")
        return w.snap("V")

    def transpose_tile_to(self, w, src_bf, dst, st_idx, wait_v=None):
        c = self.cfg
        for half in range(c.C // 8):
            bank = self.tb[half % 2]
            tag = self._tp_tags.get(("tb", half % 2))
            if tag is not None:
                w.wait("tensor", "V", tag)
            if wait_v is not None:
                w.wait("tensor", "V", wait_v)
                wait_v = None
            for j in range(8):
                cc_ = half * 8 + j
                w.ins("tensor", lambda e, cc_=cc_, j=j, bank=bank: e.transpose(
                    bank[:, j * 128:(j + 1) * 128],
                    src_bf[:, cc_ * 128:(cc_ + 1) * 128],
                    self.id_b[:, :]),
                    inc="T" if j == 7 else None)
            t_done = w.snap("T")
            w.wait("vector", "T", t_done)
            w.ins("vector", lambda e, half=half, bank=bank: e.tensor_copy(
                dst[:, half * 8:(half + 1) * 8, st_idx * 128:(st_idx + 1) * 128],
                bank[:, :].rearrange("p (j f) -> p j f", f=128)), inc="V")
            self._tp_tags[("tb", half % 2)] = w.snap("V")
        return w.snap("T")

    def wstrip_prep(self, w, src_dram, row0, slot, dst=None):
        c = self.cfg
        tag = self._tp_tags.get(("wstrip", slot))
        if tag is not None:
            w.wait("sync", "V", tag)
        lsem = "ldw0" if slot == 0 else "ldw1"
        w.ins("sync", lambda e: e.dma_start(out=self.wstrip[:, slot, :],
                                            in_=src_dram[row0:row0 + 128, :]), inc=lsem, by=16)
        ld = w.snap(lsem)
        w.wait("vector", lsem, ld)
        w.ins("vector", lambda e: e.tensor_copy(self.wbf[:, slot, :], self.wstrip[:, slot, :]), inc="V")
        self._tp_tags[("wstrip", slot)] = w.snap("V")
        cv = w.snap("V")
        for half in range(c.C // 8):
            bank = self.tb[half % 2]
            btag = self._tp_tags.get(("tb", half % 2))
            if btag is not None:
                w.wait("tensor", "V", btag)
            if half == 0:
                w.wait("tensor", "V", cv)
            for j in range(8):
                cc_ = half * 8 + j
                w.ins("tensor", lambda e, cc_=cc_, j=j, bank=bank: e.transpose(
                    bank[:, j * 128:(j + 1) * 128],
                    self.wbf[:, slot, cc_ * 128:(cc_ + 1) * 128],
                    self.id_b[:, :]),
                    inc="T" if j == 7 else None)
            td = w.snap("T")
            w.wait("vector", "T", td)
            if dst is None:
                w.ins("vector", lambda e, half=half, bank=bank: e.tensor_copy(
                    self.wT[:, slot, half * 8:(half + 1) * 8, :],
                    bank[:, :].rearrange("p (j f) -> p j f", f=128)), inc="V")
            else:
                w.ins("vector", lambda e, half=half, bank=bank: e.tensor_copy(
                    dst(half), bank[:, :].rearrange("p (j f) -> p j f", f=128)), inc="V")
            self._tp_tags[("tb", half % 2)] = w.snap("V")
        return w.snap("V")

    def ph_ln1(self, w):
        c = self.cfg
        w.ins("sync", lambda e: e.dma_start(out=self.id_b[:, :], in_=self.ident[:, :]), inc="ldpro", by=16)
        w.ins("sync", lambda e: e.dma_start(out=self.ones_f[:, :], in_=self.ones[:, :]), inc="ldpro", by=16)
        w.ins("sync", lambda e: e.dma_start(out=self.cst[:, :], in_=self.consts[:, :]), inc="ldpro", by=16)
        ld0 = w.snap("ldpro")
        for eng in ("vector", "tensor", "scalar", "gpsimd"):
            w.wait(eng, "ldpro", ld0)
        w.ins("vector", lambda e: e.tensor_copy(self.ones_b[:, :], self.ones_f[:, :]), inc="V")
        self.v_onesb = w.snap("V")
        w.wait("tensor", "V", self.v_onesb)
        for stt in range(c.NT):
            slot = stt % 2
            tag = self._tp_tags.get(("xstrip", slot))
            if tag is not None:
                w.wait("gpsimd", "V", tag)
            xsem = "ldx0" if slot == 0 else "ldx1"
            w.ins("gpsimd", lambda e, stt=stt, slot=slot: e.dma_start(
                out=self.xstrip[:, slot, :],
                in_=self.x[stt * 128:(stt + 1) * 128, :]), inc=xsem, by=16)
            ld = w.snap(xsem)
            w.wait("vector", xsem, ld)
            w.wait("scalar", xsem, ld)
            nb_war = self._tp_tags.get(("nb", slot))
            vdone = self.ln_tile(w, self.xstrip[:, slot, :], self.nb[:, slot, :],
                                 out_war=("T", nb_war) if nb_war else None, idx=slot,
                                 disc=self.eb[:, slot, 0:c.D // 2])
            self._tp_tags[("xstrip", slot)] = vdone
            t_tp = self.transpose_tile_to(w, self.nb[:, slot, :], self.xnT, stt, wait_v=vdone)
            self._tp_tags[("nb", slot)] = t_tp
        self.v_xnT_done = w.snap("V")
        if c.dbg:
            w.wait("gpsimd", "V", self.v_xnT_done)
            w.ins("gpsimd", lambda e: e.dma_start(
                out=self.dbg_xnT[:, :].rearrange("p (a b) -> p a b", b=c.T),
                in_=self.xnT[:, :, :]), inc="stc", by=16)
            w.wait("gpsimd", "stc", w.snap("stc"))

    def ph_proj(self, w):
        c = self.cfg
        scw = min(512, c.T)
        nsc = c.T // scw

        for et in range(c.E):
            slot = et % 2
            rdy = self.wstrip_prep(w, self.wk, et * 128, slot)
            for sc in range(nsc):
                pbank = self.pk[sc % 2]
                ptag = self._tp_tags.get(("pk", sc % 2))
                if ptag is not None:
                    w.wait("tensor", "V", ptag)
                if sc == 0:
                    w.wait("tensor", "V", max(rdy, self.v_xnT_done))
                for cc_ in range(c.C):
                    w.ins("tensor", lambda e, cc_=cc_, sc=sc, slot=slot, pbank=pbank: e.matmul(
                        pbank[:, 0:scw],
                        self.wT[:, slot, cc_, :],
                        self.xnT[:, cc_, sc * scw:(sc + 1) * scw],
                        start=(cc_ == 0), stop=(cc_ == c.C - 1)),
                        inc="T" if cc_ == c.C - 1 else None)
                mm = w.snap("T")
                w.wait("vector", "T", mm)
                etag = self._tp_tags.get(("eb", slot))
                if etag is not None:
                    w.wait("vector", "stc", w.snap("stc"))
                w.ins("vector", lambda e, sc=sc, slot=slot, pbank=pbank: e.tensor_copy(
                    self.eb[:, slot, sc * scw:(sc + 1) * scw], pbank[:, 0:scw]), inc="V")
                self._tp_tags[("pk", sc % 2)] = w.snap("V")
            row_done = w.snap("V")
            w.wait("gpsimd", "V", row_done)
            w.ins("gpsimd", lambda e, et=et, slot=slot: e.dma_start(
                out=self.cc_in_k[et * 128:(et + 1) * 128, :],
                in_=self.eb[:, slot, 0:c.T]), inc="stc", by=16)
            self._tp_tags[("eb", slot)] = w.snap("stc")
        st_k = w.snap("stc")
        w.wait("gpsimd", "stc", w.snap("stc"))
        w.ins("gpsimd", lambda e: e.collective_compute(
            "AllGather", ALU.bypass,
            replica_groups=self._replica_groups(),
            ins=[self.cc_in_k.ap().opt()], outs=[self.cc_out_k.ap().opt()]), inc="ccK")

        for ec in range(c.D // 512):
            for j in range(4):
                et = ec * 4 + j
                slot = et % 2
                wtv_tag = self._tp_tags.get("wTv")
                if wtv_tag is not None and j == 0:
                    w.wait("vector", "T", wtv_tag)
                self.wstrip_prep(w, self.wv, et * 128, slot,
                                 dst=lambda half, j=j: self.wTv[:, half * 8:(half + 1) * 8,
                                                               j * 128:(j + 1) * 128])
            wtv_ready = w.snap("V")
            for stt in range(c.NT):
                pbank = self.pk[stt % 2]
                ptag = self._tp_tags.get(("pk", stt % 2))
                if ptag is not None:
                    w.wait("tensor", "V", ptag)
                if stt == 0:
                    w.wait("tensor", "V", wtv_ready)
                for cc_ in range(c.C):
                    w.ins("tensor", lambda e, cc_=cc_, stt=stt, pbank=pbank: e.matmul(
                        pbank[:, :],
                        self.xnT[:, cc_, stt * 128:(stt + 1) * 128],
                        self.wTv[:, cc_, :],
                        start=(cc_ == 0), stop=(cc_ == c.C - 1)),
                        inc="T" if cc_ == c.C - 1 else None)
                mm = w.snap("T")
                self._tp_tags["wTv"] = mm
                w.wait("vector", "T", mm)
                eslot = stt % 2
                etag = self._tp_tags.get(("eb", eslot))
                if etag is not None:
                    w.wait("vector", "stc", w.snap("stc"))
                w.ins("vector", lambda e, eslot=eslot, pbank=pbank: e.tensor_copy(
                    self.eb[:, eslot, 0:512], pbank[:, :]), inc="V")
                vdone = w.snap("V")
                self._tp_tags[("pk", stt % 2)] = vdone
                w.wait("gpsimd", "V", vdone)
                w.ins("gpsimd", lambda e, stt=stt, ec=ec, eslot=eslot: e.dma_start(
                    out=self.cc_in_v[stt * 128:(stt + 1) * 128, ec * 512:(ec + 1) * 512],
                    in_=self.eb[:, eslot, 0:512]), inc="stc", by=16)
                self._tp_tags[("eb", eslot)] = w.snap("stc")
        st_v = w.snap("stc")
        w.wait("gpsimd", "stc", w.snap("stc"))
        w.ins("gpsimd", lambda e: e.collective_compute(
            "AllGather", ALU.bypass,
            replica_groups=self._replica_groups(),
            ins=[self.cc_in_v.ap().opt()], outs=[self.cc_out_v.ap().opt()]), inc="ccV")

        for et in range(c.E):
            slot = et % 2
            rdy = self.wstrip_prep(w, self.wq, et * 128, slot)
            for sc in range(nsc):
                pbank = self.pk[sc % 2]
                ptag = self._tp_tags.get(("pk", sc % 2))
                if ptag is not None:
                    w.wait("tensor", "V", ptag)
                if sc == 0:
                    w.wait("tensor", "V", rdy)
                for cc_ in range(c.C):
                    w.ins("tensor", lambda e, cc_=cc_, sc=sc, slot=slot, pbank=pbank: e.matmul(
                        pbank[:, 0:scw],
                        self.wT[:, slot, cc_, :],
                        self.xnT[:, cc_, sc * scw:(sc + 1) * scw],
                        start=(cc_ == 0), stop=(cc_ == c.C - 1)),
                        inc="T" if cc_ == c.C - 1 else None)
                mm = w.snap("T")
                w.wait("vector", "T", mm)
                w.ins("vector", lambda e, et=et, sc=sc, pbank=pbank: e.tensor_scalar(
                    out=self.QT[:, et, sc * scw:(sc + 1) * scw], in0=pbank[:, 0:scw],
                    scalar1=float(c.SCALE), scalar2=None, op0=ALU.mult), inc="V")
                self._tp_tags[("pk", sc % 2)] = w.snap("V")
        if c.dbg:
            v_q = w.snap("V")
            w.wait("gpsimd", "V", v_q)
            w.ins("gpsimd", lambda e: e.dma_start(
                out=self.dbg_qT[:, :].rearrange("p (a b) -> p a b", b=c.T),
                in_=self.QT[:, :, :]), inc="stc", by=16)
            w.wait("gpsimd", "stc", w.snap("stc"))

    def ph_attn(self, w):
        c = self.cfg
        jobs = self._gamma_jobs()
        n_iter = c.H * c.NQC
        per = min(-(-len(jobs) // n_iter), 6)
        job_idx = [0]
        gam_first = [True, True, True]
        gam_firstV = [True, True, True]
        gslot_tag = {}

        last_exp = 0
        evac_snap = {}
        zb_free = 0
        prev_zmm = [0, 0]
        pt_user = {}
        pending_fin = [None]

        def fin_stage1(w):
            st_ = pending_fin[0]
            if st_ is None or st_.get("stage") != 0:
                return
            (fh, fqc, fgi, fob, fz_last, fpar) = st_["t"]
            zb = self.ZB[fpar]
            w.wait("tensor", "V", fz_last["vector"])
            w.wait("tensor", "V", zb_free[fpar])
            for i in range(2):
                w.ins("tensor", lambda e, i=i, fpar=fpar, zb=zb: e.matmul(
                    zb[0:1, 0:c.QCW], self.ones_b[:, 0:1],
                    self.Zacc[fpar][:, i * c.QCW:(i + 1) * c.QCW],
                    start=(i == 0), stop=(i == 1), skip_group_check=True),
                    inc="T" if i == 1 else None)
            z_mm = w.snap("T")
            prev_zmm[fpar] = z_mm
            w.wait("vector", "T", z_mm)
            w.wait("vector", "T", last_bcast[0])
            w.ins("vector", self._lp(lambda e, zb=zb: e.reciprocal(out=self.rzh[0:1, :], in_=zb[0:1, 0:c.QCW])), inc="V")
            st_["v_recip"] = w.snap("V")
            st_["stage"] = 1

        def fin_stage2(w):
            st_ = pending_fin[0]
            if st_ is None or st_.get("stage") != 1:
                return
            pending_fin[0] = None
            (fh, fqc, fgi, fob, fz_last, fpar) = st_["t"]
            zb = self.ZB[fpar]
            w.wait("tensor", "V", st_["v_recip"])
            w.ins("tensor", lambda e, zb=zb: e.matmul(
                zb[:, 0:c.QCW], self.ones_b[0:1, :], self.rzh[0:1, :],
                start=True, stop=True, skip_group_check=True), inc="T")
            bc_mm = w.snap("T")
            last_bcast[0] = bc_mm
            w.wait("vector", "T", bc_mm)
            w.wait("vector", "V", last_evac[0])
            w.ins("vector", lambda e, zb=zb: e.tensor_copy(self.rz[:, :], zb[:, 0:c.QCW]), inc="V")
            zb_free[fpar] = w.snap("V")
            w.wait("vector", "T", fz_last["pv_done"])
            w.wait("vector", "V", zb_free[fpar])
            w.ins("vector", lambda e, fh=fh, fqc=fqc, fob=fob: e.tensor_tensor(
                out=self.QT[:, fh, fqc * c.QCW:(fqc + 1) * c.QCW],
                in0=fob[:, :], in1=self.rz[:, :], op=ALU.mult), inc="V")
            evac_snap[("o", fgi % 2, fgi)] = w.snap("V")
            last_evac[0] = w.snap("V")

        zb_free = [0, 0]
        last_evac = [0]
        last_bcast = [0]
        exp_of_slot = [0, 0]
        for h in range(c.H):
            hb = h % 2
            ktag = self._tp_tags.get(("kh", hb))
            if ktag is not None:
                w.wait("sync", "T", ktag)
            if h == 0:
                w.wait("sync", "ccK", 1)
            ksem = "ldk0" if hb == 0 else "ldk1"
            vsem = "ldv0" if hb == 0 else "ldv1"
            for r_ in range(c.R):
                w.ins("sync", lambda e, h=h, hb=hb, r_=r_: e.dma_start(
                    out=self.KH[:, hb, r_ * c.T:(r_ + 1) * c.T],
                    in_=self.cc_out_k[r_ * c.D + h * 128: r_ * c.D + (h + 1) * 128, :]),
                    inc=ksem, by=16)
            if h == 0:
                w.wait("sync", "ccV", 1)
            ktpr = c.KT // c.R
            for r_ in range(c.R):
                w.ins("sync", lambda e, h=h, hb=hb, r_=r_, ktpr=ktpr: e.dma_start(
                    out=self.VH[:, hb, r_ * ktpr:(r_ + 1) * ktpr, :],
                    in_=self.cc_out_v[r_ * c.T:(r_ + 1) * c.T, h * 128:(h + 1) * 128]
                        .rearrange("(t p) e -> p t e", p=128)),
                    inc=vsem, by=16)
            ldk_h = w.snap(ksem)
            ldv_h = w.snap(vsem)

            for qc in range(c.NQC):
                gi_ = h * c.NQC + qc
                par = gi_ % 2
                ob = self.Op[gi_ % 2]
                my_jobs = []
                for k_ in range(per):
                    if job_idx[0] < len(jobs):
                        my_jobs.append((job_idx[0], jobs[job_idx[0]]))
                        job_idx[0] += 1
                for ji, (gidx, apf, width) in my_jobs:
                    slot = ji % 6
                    gsem = f"ldg{slot}"
                    tag = gslot_tag.get(slot)
                    if tag is not None:
                        w.wait("gpsimd", tag[0], tag[1])
                    w.ins("gpsimd", lambda e, apf=apf, slot=slot, width=width: e.dma_start(
                        out=self.gbuf[:, slot, 0:width], in_=apf()), inc=gsem, by=16)

                def gamma_one(ji, gidx, width):
                    slot = ji % 6
                    gsem = f"ldg{slot}"
                    if ji % 2 == 0:
                        geng, gctr = "vector", "V"
                    else:
                        geng, gctr = "scalar", "S"
                    w.wait(geng, gsem, (ji // 6 + 1) * 16)
                    w.wait(geng, gctr, w.snap(gctr))
                    if geng == "scalar":
                        w.ins("scalar", lambda e, slot=slot, width=width: e.activation(
                            out=self.sqb[:, 0:width], in_=self.gbuf[:, slot, 0:width],
                            func=AF.Abs, accum_out=self.gtmp), inc="S")
                    else:
                        w.ins("vector", lambda e, slot=slot, width=width: e.tensor_reduce(
                            out=self.gtmpV, in_=self.gbuf[:, slot, 0:width],
                            axis=AX.X, op=ALU.add, apply_absolute_value=True), inc="V")
                    gslot_tag[slot] = (gctr, w.snap(gctr))
                    w.wait(geng, gctr, w.snap(gctr))
                    gt = self.gtmp if geng == "scalar" else self.gtmpV
                    ga = self.gacc if geng == "scalar" else self.gaccV
                    gfirst = gam_first if geng == "scalar" else gam_firstV
                    if gfirst[gidx]:
                        if geng == "scalar":
                            w.ins("scalar", lambda e, gidx=gidx, gt=gt, ga=ga: e.copy(ga[gidx], gt), inc="S")
                        else:
                            w.ins("vector", lambda e, gidx=gidx, gt=gt, ga=ga: e.tensor_copy(ga[gidx], gt), inc="V")
                        gfirst[gidx] = False
                    else:
                        if geng == "scalar":
                            w.ins("scalar", lambda e, gidx=gidx, gt=gt, ga=ga: e.add(ga[gidx], ga[gidx], gt), inc="S")
                        else:
                            w.ins("vector", lambda e, gidx=gidx, gt=gt, ga=ga: e.tensor_tensor(
                                out=ga[gidx], in0=ga[gidx], in1=gt, op=ALU.add), inc="V")

                first_z = {"vector": True}
                z_last = {"vector": None, "pv_done": None}
                exp_snap = {}
                pt_z_of = {}
                for g in range(c.NG + 1):
                    if g < c.NG:
                        pb = g % 6
                        sslot = g % 2
                        sp = self.Sp[sslot]
                        w.wait("tensor", "S", exp_of_slot[sslot])
                        if g == 0 and qc == 0:
                            w.wait("tensor", ksem, ldk_h)
                        if g == 0:
                            prev = evac_snap.get(("o", gi_ % 2, gi_ - 2))
                            if prev is not None:
                                w.wait("tensor", "V", prev)
                        for i in range(c.G):
                            kt = g * c.G + i
                            w.ins("tensor", lambda e, kt=kt, i=i, qc=qc, hb=hb, h=h, sp=sp: e.matmul(
                                sp[:, i * c.QCW:(i + 1) * c.QCW],
                                self.KH[:, hb, kt * 128:(kt + 1) * 128],
                                self.QT[:, h, qc * c.QCW:(qc + 1) * c.QCW],
                                start=True, stop=True, skip_group_check=True),
                                inc="T" if i == c.G - 1 else None)
                        s_mm = w.snap("T")
                        pu = pt_user.get(pb)
                        tw = s_mm if pu is None else max(s_mm, pu[0])
                        w.wait("scalar", "T", tw)
                        if pu is not None:
                            w.wait("scalar", pu[1], pu[2])
                        w.ins("scalar", lambda e, pb=pb, sp=sp: e.activation(
                            out=self.PT[:, pb, :], in_=sp[:, :], func=AF.Exp), inc="S")
                        last_exp = w.snap("S")
                        exp_of_slot[sslot] = last_exp
                        zeng = "vector"
                        zctr = "V"
                        acc = self.Zacc[par]
                        w.wait(zeng, "S", last_exp)
                        if z_last[zeng] is not None:
                            w.wait(zeng, zctr, z_last[zeng])
                        if first_z[zeng]:
                            w.wait(zeng, "T", prev_zmm[par])
                            w.ins(zeng, lambda e, pb=pb, acc=acc: e.tensor_copy(
                                acc[:, :], self.PT[:, pb, :]), inc=zctr)
                            first_z[zeng] = False
                        else:
                            w.ins(zeng, self._lp(lambda e, pb=pb, acc=acc: e.tensor_tensor(
                                out=acc[:, :], in0=acc[:, :],
                                in1=self.PT[:, pb, :], op=ALU.add)), inc=zctr)
                        z_last[zeng] = w.snap(zctr)
                        pt_z = (zctr, z_last[zeng])
                    if g < c.NG:
                        for _ji in range(len(my_jobs)):
                            if g == min(3 + 6 * _ji, c.NG - 1):
                                gamma_one(my_jobs[_ji][0], my_jobs[_ji][1][0], my_jobs[_ji][1][2])
                    if g == min(1, c.NG - 1):
                        fin_stage1(w)
                    if g == min(8, c.NG):
                        fin_stage2(w)
                    gp = g - 1
                    if gp >= 0:
                        pbp = gp % 6
                        w.wait("tensor", "S", exp_snap[gp])
                        if gp == 0 and qc == 0:
                            w.wait("tensor", vsem, ldv_h)
                        for i in range(c.G):
                            kt = gp * c.G + i
                            w.ins("tensor", lambda e, kt=kt, i=i, hb=hb, ob=ob, pbp=pbp: e.matmul(
                                ob[:, :],
                                self.VH[:, hb, kt, :],
                                self.PT[:, pbp, i * c.QCW:(i + 1) * c.QCW],
                                start=(kt == 0), stop=(kt == c.KT - 1),
                                skip_group_check=True),
                                inc="T" if i == c.G - 1 else None)
                        pt_user[pbp] = (w.snap("T"), pt_z_of[gp][0], pt_z_of[gp][1])
                    if g < c.NG:
                        exp_snap[g] = last_exp
                        pt_z_of[g] = pt_z
                z_last["pv_done"] = w.snap("T")
                pending_fin[0] = {"t": (h, qc, gi_, ob, dict(z_last), par), "stage": 0}
            self._tp_tags[("kh", hb)] = w.snap("T")
        fin_stage1(w)
        fin_stage2(w)

        while job_idx[0] < len(jobs):
            ji = job_idx[0]
            gidx, apf, width = jobs[job_idx[0]]
            job_idx[0] += 1
            slot = ji % 6
            gsem = f"ldg{slot}"
            tag = gslot_tag.get(slot)
            if tag is not None:
                w.wait("gpsimd", tag[0], tag[1])
            w.ins("gpsimd", lambda e, apf=apf, slot=slot, width=width: e.dma_start(
                out=self.gbuf[:, slot, 0:width], in_=apf()), inc=gsem, by=16)
            w.wait("scalar", gsem, (ji // 6 + 1) * 16)
            w.wait("scalar", "S", w.snap("S"))
            w.ins("scalar", lambda e, slot=slot, width=width: e.activation(
                out=self.sqb[:, 0:width], in_=self.gbuf[:, slot, 0:width],
                func=AF.Abs, accum_out=self.gtmp), inc="S")
            gslot_tag[slot] = ("S", w.snap("S"))
            w.wait("scalar", "S", w.snap("S"))
            if gam_first[gidx]:
                w.ins("scalar", lambda e, gidx=gidx: e.copy(self.gacc[gidx], self.gtmp), inc="S")
                gam_first[gidx] = False
            else:
                w.ins("scalar", lambda e, gidx=gidx: e.add(
                    self.gacc[gidx], self.gacc[gidx], self.gtmp), inc="S")
        self.g_done = w.snap("S")

        self.emit_gamma_finalize(w)
        if c.dbg:
            v_all = w.snap("V")
            w.wait("gpsimd", "V", v_all)
            w.ins("gpsimd", lambda e: e.dma_start(
                out=self.dbg_att[:, :].rearrange("p (a b) -> p a b", b=c.T),
                in_=self.QT[:, :, :]), inc="stc", by=16)
            w.wait("gpsimd", "stc", w.snap("stc"))
            for gi in range(3):
                w.ins("gpsimd", lambda e, gi=gi: e.dma_start(
                    out=self.dbg_thr[:, gi:gi + 1], in_=self.thr[gi],
                    allow_slow_non_contiguous=True), inc="stc", by=16)
            w.wait("gpsimd", "stc", w.snap("stc"))

    def _gamma_jobs(self):
        c = self.cfg
        jobs = []
        nw = c.D // 1024
        for mi, mat in enumerate([self.w1, self.w2]):
            for j in range(c.HF // 128):
                for k in range(nw):
                    jobs.append((mi, lambda mat=mat, j=j, k=k: mat[j * 128:(j + 1) * 128, k * 1024:(k + 1) * 1024], 1024))
        hq = c.HF // 8
        assert hq <= 1024
        for j in range(c.D // 128):
            for k in range(8):
                jobs.append((2, lambda j=j, k=k: self.w3[j * 128:(j + 1) * 128, k * hq:(k + 1) * hq], hq))
        return jobs

    def emit_gamma_finalize(self, w):
        c = self.cfg
        n_el = float(c.HF * c.D)
        for gi in range(3):
            w.wait("vector", "S", self.g_done)
            w.wait("vector", "V", w.snap("V"))
            w.ins("vector", lambda e, gi=gi: e.tensor_tensor(
                out=self.gaccV[gi], in0=self.gaccV[gi], in1=self.gacc[gi], op=ALU.add), inc="V")
            w.wait("tensor", "V", w.snap("V"))
            w.ins("tensor", lambda e, gi=gi: e.matmul(
                self.ZB[0][0:1, 0:1], self.ones_f[:, 0:1], self.gaccV[gi],
                start=True, stop=True, skip_group_check=True), inc="T")
            t0 = w.snap("T")
            w.wait("scalar", "T", t0)
            w.wait("scalar", "S", w.snap("S"))
            w.ins("scalar", lambda e, gi=gi: e.mul(self.std[0:1, :], self.ZB[0][0:1, 0:1], 0.5 / n_el), inc="S")
            w.wait("scalar", "S", w.snap("S"))
            w.ins("scalar", lambda e: e.add(self.std[0:1, :], self.std[0:1, :], self.cst[0:1, 1:2]), inc="S")
            s1 = w.snap("S")
            w.wait("tensor", "S", s1)
            w.wait("tensor", "V", w.snap("V"))
            w.ins("tensor", lambda e: e.matmul(
                self.ZB[0][:, 0:1], self.ones_f[0:1, :], self.std[0:1, :],
                start=True, stop=True, skip_group_check=True), inc="T")
            t1 = w.snap("T")
            w.wait("vector", "T", t1)
            w.ins("vector", lambda e, gi=gi: e.tensor_copy(self.thr[gi], self.ZB[0][:, 0:1]), inc="V")
            w.wait("vector", "V", w.snap("V"))
            w.ins("vector", lambda e, gi=gi: e.tensor_scalar(
                out=self.nthr[gi], in0=self.thr[gi],
                scalar1=-1.0, scalar2=None, op0=ALU.mult), inc="V")

    def ph_outproj(self, w):
        c = self.cfg
        for et in range(c.E):
            slot = et % 2
            self.wstrip_prep(w, self.wo, et * 128, slot,
                             dst=lambda half, et=et: self.wTo[:, half * 8:(half + 1) * 8,
                                                             et * 128:(et + 1) * 128])
        wto_ready = w.snap("V")

        ev_tag = 0
        for stt in range(c.NT):
            slot = stt % 2
            tag = self._tp_tags.get(("xstrip", slot))
            if tag is not None:
                w.wait("gpsimd", "V", tag)
            xsem = "ldx0" if slot == 0 else "ldx1"
            w.ins("gpsimd", lambda e, stt=stt, slot=slot: e.dma_start(
                out=self.xstrip[:, slot, :],
                in_=self.x[stt * 128:(stt + 1) * 128, :]), inc=xsem, by=16)
            ld = w.snap(xsem)
            if stt == 0:
                w.wait("tensor", "V", wto_ready)
            else:
                w.wait("tensor", "V", ev_tag)
            for cc_ in range(c.C):
                for ec in range(c.D // 512):
                    w.ins("tensor", lambda e, cc_=cc_, stt=stt, ec=ec: e.matmul(
                        self.poC[:, ec * 512:(ec + 1) * 512],
                        self.QT[:, cc_, stt * 128:(stt + 1) * 128],
                        self.wTo[:, cc_, ec * 512:(ec + 1) * 512],
                        start=(cc_ == 0), stop=(cc_ == c.C - 1), skip_group_check=True),
                        inc="T" if (cc_ == c.C - 1 and ec == c.D // 512 - 1) else None)
            mm = w.snap("T")
            w.wait("vector", "T", mm)
            w.wait("vector", xsem, ld)
            x1tag = self._tp_tags.get("x1f")
            if x1tag is not None:
                w.wait("vector", "V", x1tag)
            w.ins("vector", lambda e, slot=slot: e.tensor_tensor(
                out=self.x1f[:, :], in0=self.xstrip[:, slot, :],
                in1=self.poC[:, :], op=ALU.add), inc="V")
            v_x1 = w.snap("V")
            ev_tag = v_x1
            self._tp_tags[("xstrip", slot)] = v_x1
            w.wait("vector", "V", v_x1)
            w.wait("vector", "stc", w.snap("stc"))
            w.ins("vector", lambda e: e.tensor_copy(self.x1c[:, :], self.x1f[:, :]), inc="V")
            v_x1c = w.snap("V")
            w.wait("gpsimd", "V", v_x1c)
            w.ins("gpsimd", lambda e, stt=stt: e.dma_start(
                out=self.x1_dram[:, stt * c.D:(stt + 1) * c.D], in_=self.x1c[:, :]), inc="stc", by=16)
            w.wait("scalar", "V", v_x1)
            nb_war = ("stc", w.snap("stc")) if stt > 1 else None
            v_ln = self.ln_tile(w, self.x1f[:, :], self.nb[:, slot, :], out_war=nb_war, idx=slot,
                                disc=self.wbf[:, slot, 0:c.D // 2])
            self._tp_tags["x1f"] = v_ln
            w.wait("gpsimd", "V", v_ln)
            w.ins("gpsimd", lambda e, stt=stt, slot=slot: e.dma_start(
                out=self.xn2_dram[:, stt * c.D:(stt + 1) * c.D], in_=self.nb[:, slot, :]), inc="stc", by=16)

    def ph_ffn1(self, w):
        c = self.cfg
        for stt in range(c.NT):
            slot = stt % 2
            ntag = self._tp_tags.get(("nb", slot))
            if ntag is not None:
                w.wait("sync", "T", ntag)
            hsem = "ldh0" if slot == 0 else "ldh1"
            w.ins("sync", lambda e, stt=stt, slot=slot: e.dma_start(
                out=self.nb[:, slot, :], in_=self.xn2_dram[:, stt * c.D:(stt + 1) * c.D]), inc=hsem, by=16)
            ld = w.snap(hsem)
            w.wait("tensor", hsem, ld)
            t_tp = self.transpose_tile_to(w, self.nb[:, slot, :], self.xn2T, stt, wait_v=None)
            self._tp_tags[("nb", slot)] = t_tp
        xn2T_done = w.snap("V")

        w3_q_at = {}
        step = max(1, c.HT // c.E)
        for k in range(c.E):
            w3_q_at.setdefault(min(k * step, c.HT - 2), []).append(k)
        w3_pending = []

        def prep_q(ht):
            for im, (mat, gi) in enumerate([(self.w1, 0), (self.w2, 1)]):
                slot = im
                tag = self._tp_tags.get(("wstrip2", slot))
                if tag is not None:
                    w.wait("sync", "V", tag)
                lsem = "ldw0" if slot == 0 else "ldw1"
                w.ins("sync", lambda e, mat=mat, ht=ht, slot=slot: e.dma_start(
                    out=self.wstrip2[:, slot, :], in_=mat[ht * 128:(ht + 1) * 128, :]), inc=lsem, by=16)
                ld = w.snap(lsem)
                w.wait("vector", lsem, ld)
                w.wait("vector", "V", w.snap("V"))
                qtag = self._tp_tags.get(("qbf", slot))
                if qtag is not None:
                    w.wait("vector", "T", qtag)
                w.ins("vector", lambda e, slot=slot, gi=gi: e.tensor_scalar(
                    out=self.nb[:, 0, :], in0=self.wstrip2[:, slot, :],
                    scalar1=self.thr[gi], scalar2=None, op0=ALU.is_ge), inc="V")
                w.ins("vector", lambda e, slot=slot, gi=gi: e.tensor_scalar(
                    out=self.sgnb[:, :], in0=self.wstrip2[:, slot, :],
                    scalar1=self.nthr[gi], scalar2=None, op0=ALU.is_ge), inc="V")
                vb = w.snap("V")
                w.wait("vector", "V", vb)
                w.ins("vector", lambda e, slot=slot: e.scalar_tensor_tensor(
                    out=self.qbf[:, slot, :], in0=self.nb[:, 0, :], scalar=-1.0,
                    in1=self.sgnb[:, :], op0=ALU.add, op1=ALU.add), inc="V")
                self._tp_tags[("wstrip2", slot)] = w.snap("V")
            return w.snap("V")

        def prep_t(ht, q_ready):
            bslot = ht % 2
            for im, qT in enumerate([self.q1T, self.q2T]):
                slot = im
                for half in range(c.C // 8):
                    bank = self.tb[half % 2]
                    btag = self._tp_tags.get(("tb", half % 2))
                    if btag is not None:
                        w.wait("tensor", "V", btag)
                    if half == 0:
                        w.wait("tensor", "V", q_ready)
                        utag = self._tp_tags.get(("qTuse", bslot))
                        if utag is not None:
                            w.wait("tensor", "T", utag)
                    for j in range(8):
                        cc_ = half * 8 + j
                        w.ins("tensor", lambda e, cc_=cc_, j=j, bank=bank, slot=slot: e.transpose(
                            bank[:, j * 128:(j + 1) * 128],
                            self.qbf[:, slot, cc_ * 128:(cc_ + 1) * 128],
                            self.id_b[:, :]),
                            inc="T" if j == 7 else None)
                    td = w.snap("T")
                    self._tp_tags[("qbf", slot)] = td
                    w.wait("vector", "T", td)
                    w.ins("vector", lambda e, half=half, bank=bank, qT=qT, bslot=bslot: e.tensor_copy(
                        qT[:, bslot, half * 8:(half + 1) * 8, :],
                        bank[:, :].rearrange("p (j f) -> p j f", f=128)), inc="V")
                    self._tp_tags[("tb", half % 2)] = w.snap("V")
            return [("V", w.snap("V"))]

        qv = prep_q(0)
        prep_ready = {0: prep_t(0, qv)}
        for ht in range(c.HT):
            bslot = ht % 2
            if ht + 1 < c.HT:
                qv = prep_q(ht + 1)
            for sc in range(c.NSC):
                p1 = self.pfD[(2 * sc) % 4]
                p2 = self.pfD[(2 * sc + 1) % 4]
                pt1 = self._tp_tags.get(("pf", (2 * sc) % 4))
                if pt1 is not None:
                    w.wait("tensor", "V", pt1)
                pt2 = self._tp_tags.get(("pf", (2 * sc + 1) % 4))
                if pt2 is not None:
                    w.wait("tensor", "V", pt2)
                if sc == 0:
                    w.wait("tensor", "V", xn2T_done)
                    for tsem, tcnt in prep_ready[ht]:
                        w.wait("tensor", tsem, tcnt)
                for cc_ in range(c.C):
                    w.ins("tensor", lambda e, cc_=cc_, sc=sc, p1=p1, bslot=bslot: e.matmul(
                        p1[:, 0:c.SCW], self.q1T[:, bslot, cc_, :],
                        self.xn2T[:, cc_, sc * c.SCW:(sc + 1) * c.SCW],
                        start=(cc_ == 0), stop=(cc_ == c.C - 1)),
                        inc="T" if cc_ == c.C - 1 else None)
                mm1 = w.snap("T")
                for cc_ in range(c.C):
                    w.ins("tensor", lambda e, cc_=cc_, sc=sc, p2=p2, bslot=bslot: e.matmul(
                        p2[:, 0:c.SCW], self.q2T[:, bslot, cc_, :],
                        self.xn2T[:, cc_, sc * c.SCW:(sc + 1) * c.SCW],
                        start=(cc_ == 0), stop=(cc_ == c.C - 1)),
                        inc="T" if cc_ == c.C - 1 else None)
                mm2 = w.snap("T")
                self._tp_tags[("qTuse", bslot)] = mm2
                w.wait("scalar", "T", mm1)
                stag = self._tp_tags.get(("sig", sc % 2))
                if stag is not None:
                    w.wait("scalar", "V", stag)
                w.ins("scalar", lambda e, sc=sc, p1=p1: e.activation(
                    out=self.sig[:, sc % 2, 0:c.SCW], in_=p1[:, 0:c.SCW], func=AF.Sigmoid), inc="S")
                s_sig = w.snap("S")
                w.wait("vector", "S", s_sig)
                w.wait("vector", "T", mm2)
                htag = self._tp_tags.get(("hidb", sc % 2))
                if htag is not None:
                    w.wait("vector", "sth0" if sc % 2 == 0 else "sth1", htag)
                w.ins("vector", lambda e, sc=sc, p2=p2: e.tensor_tensor(
                    out=self.hidb[:, sc % 2, 0:c.SCW], in0=p2[:, 0:c.SCW],
                    in1=self.sig[:, sc % 2, 0:c.SCW], op=ALU.mult), inc="V")
                v_hid = w.snap("V")
                self._tp_tags[("sig", sc % 2)] = v_hid
                self._tp_tags[("pf", (2 * sc) % 4)] = v_hid
                self._tp_tags[("pf", (2 * sc + 1) % 4)] = v_hid
                w.wait("gpsimd", "V", v_hid)
                w.ins("gpsimd", lambda e, sc=sc, ht=ht: e.dma_start(
                    out=self.hid_dram[sc * 128:(sc + 1) * 128, ht * c.SCW:(ht + 1) * c.SCW],
                    in_=self.hidb[:, sc % 2, 0:c.SCW]), inc="sth0" if sc % 2 == 0 else "sth1", by=16)
                self._tp_tags[("hidb", sc % 2)] = w.snap("sth0" if sc % 2 == 0 else "sth1")

            if ht + 1 < c.HT:
                prep_ready[ht + 1] = prep_t(ht + 1, qv)
            if w3_pending:
                k, qr = w3_pending.pop()
                self.emit_w3_tp(w, k, qr)
            due = w3_q_at.get(ht, [])
            for k in due[:-1]:
                self.emit_w3_tp(w, k, self.emit_w3_quant(w, k))
            if due:
                w3_pending.append((due[-1], self.emit_w3_quant(w, due[-1])))
        while w3_pending:
            k, qr = w3_pending.pop()
            self.emit_w3_tp(w, k, qr)

    def emit_w3_quant(self, w, et):
        c = self.cfg
        tag = self._tp_tags.get("w3f")
        if tag is not None:
            w.wait("sync", "V", tag)
        w.ins("sync", lambda e, et=et: e.dma_start(
            out=self.w3f[:, :], in_=self.w3[et * 128:(et + 1) * 128, :]), inc="ldw3", by=16)
        ld = w.snap("ldw3")
        w.wait("vector", "ldw3", ld)
        gtag = self._tp_tags.get("geb")
        if gtag is not None:
            w.wait("vector", "V", gtag)
        w.ins("vector", lambda e: e.tensor_scalar(
            out=self.geb[:, :], in0=self.w3f[:, :],
            scalar1=self.thr[2], scalar2=None, op0=ALU.is_ge), inc="V")
        q3tag = self._tp_tags.get("q3s")
        if q3tag is not None:
            w.wait("vector", "T", q3tag)
        w.ins("vector", lambda e: e.tensor_scalar(
            out=self.q3s[:, :], in0=self.w3f[:, :],
            scalar1=self.nthr[2], scalar2=None, op0=ALU.is_ge), inc="V")
        vb = w.snap("V")
        self._tp_tags["w3f"] = vb
        w.wait("vector", "V", vb)
        w.ins("vector", lambda e: e.scalar_tensor_tensor(
            out=self.q3s[:, :], in0=self.geb[:, :], scalar=-1.0,
            in1=self.q3s[:, :], op0=ALU.add, op1=ALU.add), inc="V")
        self._tp_tags["geb"] = w.snap("V")
        return w.snap("V")

    def emit_w3_tp(self, w, et, q_ready):
        c = self.cfg
        stg_tag = self._tp_tags.get("q3stage")
        for bi in range(c.HT):
            bank = self.tb[(bi // 8) % 2]
            j = bi % 8
            if j == 0:
                btag = self._tp_tags.get(("tb", (bi // 8) % 2))
                if btag is not None:
                    w.wait("tensor", "V", btag)
                if bi == 0:
                    w.wait("tensor", "V", q_ready)
                    if stg_tag is not None:
                        w.wait("tensor", "stc", w.snap("stc"))
            last = (j == 7) or (bi == c.HT - 1)
            w.ins("tensor", lambda e, bi=bi, j=j, bank=bank: e.transpose(
                bank[:, j * 128:(j + 1) * 128],
                self.q3s[:, bi * 128:(bi + 1) * 128],
                self.id_b[:, :]),
                inc="T" if last else None)
            if last:
                td = w.snap("T")
                self._tp_tags["q3s"] = td
                w.wait("vector", "T", td)
                lo = (bi // 8) * 8
                hi = bi + 1
                w.ins("vector", lambda e, lo=lo, hi=hi, bank=bank: e.tensor_copy(
                    self.q3stage[:, lo:hi, :],
                    bank[:, 0:(hi - lo) * 128].rearrange("p (j f) -> p j f", f=128)), inc="V")
                self._tp_tags[("tb", (bi // 8) % 2)] = w.snap("V")
        v_stage = w.snap("V")
        w.wait("gpsimd", "V", v_stage)
        w.ins("gpsimd", lambda e, et=et: e.dma_start(
            out=self.q3t_dram[et * 128:(et + 1) * 128, :].rearrange("p (t f) -> p t f", f=128),
            in_=self.q3stage[:, :, :]), inc="stc", by=16)
        self._tp_tags["q3stage"] = w.snap("stc")

    def ph_ffn2(self, w):
        c = self.cfg
        NTL = c.SCW // 128
        for sc in range(c.NSC):
            htag = self._tp_tags.get("hid")
            if htag is not None:
                w.wait("sync", "T", htag)
            w.ins("sync", lambda e, sc=sc: e.dma_start(
                out=self.hid[:, :, :],
                in_=self.hid_dram[sc * 128:(sc + 1) * 128, :]
                    .rearrange("p (t f) -> p t f", f=c.SCW)), inc="ldh", by=16)
            ld_hid = w.snap("ldh")
            otag = self._tp_tags.get("outsb")
            if otag is not None:
                w.wait("gpsimd", "stc", w.snap("stc"))
            w.ins("gpsimd", lambda e, sc=sc: e.dma_start(
                out=self.outsb[:, :, :],
                in_=self.x1_dram[:, sc * NTL * c.D:(sc + 1) * NTL * c.D]
                    .rearrange("p (t f) -> p t f", f=c.D)), inc="ldg", by=16)
            ld_x1 = w.snap("ldg")
            for et in range(c.E):
                qslot = et % 2
                qtag = self._tp_tags.get(("q3rd", qslot))
                if qtag is not None:
                    w.wait("sync", "T", qtag)
                qsem = "ldq0" if qslot == 0 else "ldq1"
                w.ins("sync", lambda e, et=et, qslot=qslot: e.dma_start(
                    out=self.q3rd[:, qslot, :, :],
                    in_=self.q3t_dram[et * 128:(et + 1) * 128, :]
                        .rearrange("p (t f) -> p t f", f=128)), inc=qsem, by=16)
                ld_q3 = w.snap(qsem)
                ab = self.accE[et % 2]
                atag = self._tp_tags.get(("acc", et % 2))
                if atag is not None:
                    w.wait("tensor", "V", atag)
                w.wait("tensor", qsem, ld_q3)
                if et == 0:
                    w.wait("tensor", "ldh", ld_hid)
                for ht in range(c.HT):
                    w.ins("tensor", lambda e, ht=ht, qslot=qslot, ab=ab: e.matmul(
                        ab[:, 0:c.SCW], self.q3rd[:, qslot, ht, :],
                        self.hid[:, ht, :],
                        start=(ht == 0), stop=(ht == c.HT - 1)),
                        inc="T" if ht == c.HT - 1 else None)
                mm = w.snap("T")
                self._tp_tags[("q3rd", qslot)] = mm
                self._tp_tags["hid"] = mm
                w.wait("vector", "T", mm)
                pan_tag = self._tp_tags.get(("panel", et % 2))
                if pan_tag is not None:
                    w.wait("vector", "T", pan_tag)
                w.ins("vector", lambda e, et=et, ab=ab: e.tensor_copy(
                    self.panel[:, et % 2, 0:c.SCW], ab[:, 0:c.SCW]), inc="V")
                v_pan = w.snap("V")
                self._tp_tags[("acc", et % 2)] = v_pan
                tz = self.tpzE[et % 2]
                tztag = self._tp_tags.get(("tpz", et % 2))
                if tztag is not None:
                    w.wait("tensor", "V", tztag)
                w.wait("tensor", "V", v_pan)
                for j in range(NTL):
                    w.ins("tensor", lambda e, j=j, et=et, tz=tz: e.transpose(
                        tz[:, j * 128:(j + 1) * 128],
                        self.panel[:, et % 2, j * 128:(j + 1) * 128],
                        self.id_b[:, :]),
                        inc="T" if j == NTL - 1 else None)
                t_tz = w.snap("T")
                self._tp_tags[("panel", et % 2)] = t_tz
                w.wait("vector", "T", t_tz)
                if et == 0:
                    w.wait("vector", "ldg", ld_x1)
                for j in range(NTL):
                    w.ins("vector", lambda e, j=j, et=et, tz=tz: e.tensor_tensor(
                        out=self.outsb[:, j, et * 128:(et + 1) * 128],
                        in0=self.outsb[:, j, et * 128:(et + 1) * 128],
                        in1=tz[:, j * 128:(j + 1) * 128], op=ALU.add), inc="V")
                self._tp_tags[("tpz", et % 2)] = w.snap("V")
            v_out = w.snap("V")
            w.wait("gpsimd", "V", v_out)
            for j in range(NTL):
                w.ins("gpsimd", lambda e, j=j, sc=sc: e.dma_start(
                    out=self.out[(sc * NTL + j) * 128:(sc * NTL + j + 1) * 128, :],
                    in_=self.outsb[:, j, :]), inc="stc", by=16)
            self._tp_tags["outsb"] = w.snap("stc")
        w.wait("gpsimd", "stc", w.snap("stc"))


def build_nc(cfg):
    b = Builder(cfg)
    nc = b.nc
    b.alloc_rest()

    sems = {}
    sem_names = ["bar", "ldpro", "ldx0", "ldx1", "ldw0", "ldw1", "ldw3", "ldh", "ldh0", "ldh1",
                 "ldq0", "ldq1", "ldk0", "ldk1", "ldv0", "ldv1", "ldg", "ldg0", "ldg1", "ldg2", "ldg3", "ldg4", "ldg5", "sth0", "sth1", "sq0", "sq1", "tq0", "tq1",
                 "stc", "ccK", "ccV", "V", "T", "S", "G"]
    sem_stack = contextlib.ExitStack()
    for name in sem_names:
        sems[name] = sem_stack.enter_context(nc.semaphore(name))

    with nc.Block() as block:

        @block.sync
        def _(e):
            b.emit(Walk("sync", e, sems))

        @block.vector
        def _(e):
            b.emit(Walk("vector", e, sems))

        @block.tensor
        def _(e):
            b.emit(Walk("tensor", e, sems))

        @block.scalar
        def _(e):
            b.emit(Walk("scalar", e, sems))

        @block.gpsimd
        def _(e):
            b.emit(Walk("gpsimd", e, sems))

    b.close_all()
    sem_stack.close()
    return nc


def make_inputs(cfg, x, wq, wk, wv, wo, w1, w2, w3):
    c = cfg
    xf = np.ascontiguousarray(np.asarray(x, dtype=np.float32).reshape(c.B * c.S, c.D))
    ident = np.eye(128, dtype=np.float16)
    ones = np.ones((128, 128), dtype=np.float32)
    consts = np.tile(np.array([[1e-5, 0.5e-8]], dtype=np.float32), (128, 1))
    common = dict(
        wq=np.ascontiguousarray(np.asarray(wq, np.float32)),
        wk=np.ascontiguousarray(np.asarray(wk, np.float32)),
        wv=np.ascontiguousarray(np.asarray(wv, np.float32)),
        wo=np.ascontiguousarray(np.asarray(wo, np.float32)),
        w1=np.ascontiguousarray(np.asarray(w1, np.float32)),
        w2=np.ascontiguousarray(np.asarray(w2, np.float32)),
        w3=np.ascontiguousarray(np.asarray(w3, np.float32)),
        ident=ident, ones=ones, consts=consts,
    )
    return [dict(common, x=np.ascontiguousarray(xf[i * c.T:(i + 1) * c.T]))
            for i in range(c.ncores)]


def kernel(L0_attn_x, L0_attn_w_q, L0_attn_w_k, L0_attn_w_v, L0_attn_w_o,
           L0_ffn_w1, L0_ffn_w2, L0_ffn_w3):
    cfg = Cfg()
    nc = build_nc(cfg)
    in_maps = make_inputs(cfg, L0_attn_x, L0_attn_w_q, L0_attn_w_k, L0_attn_w_v,
                          L0_attn_w_o, L0_ffn_w1, L0_ffn_w2, L0_ffn_w3)
    res = run_bass_kernel_spmd(nc, in_maps, core_ids=list(range(cfg.ncores)))
    shards = [np.asarray(res.results[i]["out"], dtype=np.float32).reshape(cfg.T, cfg.D)
              for i in range(cfg.ncores)]
    return np.concatenate(shards, axis=0).reshape(cfg.B, cfg.S, cfg.D)


# version 21
# speedup vs baseline: 1.0043x; 1.0043x over previous
import contextlib
from collections import defaultdict

import numpy as np
import ml_dtypes
import concourse.bass as bass
import concourse.mybir as mybir
from concourse.bass_utils import run_bass_kernel_spmd

F32 = mybir.dt.float32
BF16 = mybir.dt.float16
AF = mybir.ActivationFunctionType
ALU = mybir.AluOpType
AX = mybir.AxisListType

ENGINES = ("sync", "vector", "tensor", "scalar", "gpsimd")


class Cfg:
    def __init__(self, B=2, S=8192, D=2048, H=16, HF=5632, ncores=8, dbg=False):
        self.B, self.S, self.D, self.H, self.HF = B, S, D, H, HF
        self.ncores = ncores
        self.R = ncores // B
        self.T = B * S // ncores
        self.NT = self.T // 128
        self.C = D // 128
        self.E = D // 128
        self.HT = HF // 128
        self.KT = S // 128
        self.DH = D // H
        assert self.DH == 128
        self.QCW = min(512, self.T)
        self.NQC = self.T // self.QCW
        self.G = 2
        assert self.KT % self.G == 0
        self.NG = self.KT // self.G
        self.SCW = min(512, self.T)
        self.NSC = self.T // self.SCW
        self.SCALE = 1.0 / float(np.sqrt(np.float64(self.DH)))
        self.dbg = dbg


class Walk:

    def __init__(self, name, eng, sems):
        self.n = name
        self.e = eng
        self.h = sems
        self.c = defaultdict(int)

    def ins(self, eng, f, inc=None, by=1):
        if self.n == eng:
            r = f(self.e)
            if inc is not None:
                r.then_inc(self.h[inc], by)
        if inc is not None:
            self.c[inc] += by

    def wait(self, eng, sem, val):
        if val is None or val <= 0:
            return
        if self.n == eng:
            self.e.wait_ge(self.h[sem], val)

    def snap(self, sem):
        return self.c[sem]

    def barrier(self):
        for eng in ENGINES:
            self.ins(eng, lambda e: e.nop(), inc="bar")
        v = self.c["bar"]
        for eng in ENGINES:
            self.wait(eng, "bar", v)
            for s_ in ("stc", "sth0", "sth1"):
                self.wait(eng, s_, self.c[s_])


class Builder:
    def __init__(self, cfg):
        self.cfg = cfg
        nc = bass.Bass("TRN2", target_bir_lowering=False)
        self.nc = nc
        c = cfg

        dt = nc.dram_tensor
        self.x = dt("x", [c.T, c.D], F32, kind="ExternalInput")
        self.wq = dt("wq", [c.D, c.D], F32, kind="ExternalInput")
        self.wk = dt("wk", [c.D, c.D], F32, kind="ExternalInput")
        self.wv = dt("wv", [c.D, c.D], F32, kind="ExternalInput")
        self.wo = dt("wo", [c.D, c.D], F32, kind="ExternalInput")
        self.w1 = dt("w1", [c.HF, c.D], F32, kind="ExternalInput")
        self.w2 = dt("w2", [c.HF, c.D], F32, kind="ExternalInput")
        self.w3 = dt("w3", [c.D, c.HF], F32, kind="ExternalInput")
        self.ident = dt("ident", [128, 128], BF16, kind="ExternalInput")
        self.consts = dt("consts", [128, 2], F32, kind="ExternalInput")
        self.ones = dt("ones", [128, 128], F32, kind="ExternalInput")
        self.out = dt("out", [c.T, c.D], F32, kind="ExternalOutput")

        self.cc_in_k = dt("cc_in_k", [c.D, c.T], BF16)
        self.cc_out_k = dt("cc_out_k", [c.R * c.D, c.T], BF16)
        self.cc_in_v = dt("cc_in_v", [c.T, c.D], BF16)
        self.cc_out_v = dt("cc_out_v", [c.R * c.T, c.D], BF16)
        self.x1_dram = dt("x1_dram", [128, c.NT * c.D], BF16)
        self.xn2_dram = dt("xn2_dram", [128, c.NT * c.D], BF16)
        self.hid_dram = dt("hid_dram", [c.NSC * 128, c.HT * c.SCW], BF16)
        self.q3t_dram = dt("q3t_dram", [c.E * 128, c.HT * 128], BF16)
        self.q12_dram = [dt(f"q12_dram{i}", [128, c.D], BF16) for i in range(2)]
        if c.dbg:
            self.dbg_xnT = dt("dbg_xnT", [128, c.C * c.T], BF16, kind="ExternalOutput")
            self.dbg_qT = dt("dbg_qT", [128, c.C * c.T], BF16, kind="ExternalOutput")
            self.dbg_att = dt("dbg_att", [128, c.C * c.T], BF16, kind="ExternalOutput")
            self.dbg_thr = dt("dbg_thr", [128, 3], F32, kind="ExternalOutput")

        st = nc.sbuf_tensor
        self.root = contextlib.ExitStack()
        r = self.root.enter_context
        self.id_b = r(st("id_b", [128, 128], BF16))
        self.ones_b = r(st("ones_b", [128, 128], BF16))
        self.cst = r(st("cst", [128, 2], F32))
        self.ones_f = r(st("ones_f", [128, 128], F32))
        self.xstrip = r(st("xstrip", [128, 2, c.D], BF16))
        self.sgnb = r(st("sgnb", [128, c.D], BF16))
        self.nb = r(st("nb", [128, 2, c.D], BF16))
        self.sqb = r(st("sqb", [128, 1024], BF16))
        self.smalls = r(st("smalls", [128, 34], F32))
        sm = self.smalls
        self.sx = sm[:, 0:1]
        self.ssq = sm[:, 1:2]
        self.mn = sm[:, 2:3]
        self.m2 = sm[:, 3:4]
        self.var = sm[:, 4:5]
        self.std = sm[:, 5:6]
        self.rstd = sm[:, 6:7]
        self.gtmp = sm[:, 7:8]
        self.gacc = [sm[:, 8 + i:9 + i] for i in range(3)]
        self.thr = [sm[:, 11 + i:12 + i] for i in range(3)]
        self.nthr = [sm[:, 14 + i:15 + i] for i in range(3)]
        self.gtmpV = sm[:, 17:18]
        self.gaccV = [sm[:, 18 + i:19 + i] for i in range(3)]
        self.lns = []
        for b_ in range(2):
            o_ = 21 + 6 * b_
            self.lns.append(dict(sx=sm[:, o_:o_ + 1], ssq=sm[:, o_ + 1:o_ + 2],
                                 mn=sm[:, o_ + 2:o_ + 3], m2=sm[:, o_ + 3:o_ + 4],
                                 var=sm[:, o_ + 4:o_ + 5], rstd=sm[:, o_ + 5:o_ + 6]))

        self.scope_qt = contextlib.ExitStack()
        self.QT = self.scope_qt.enter_context(st("QT", [128, c.C, c.T], BF16))

        self.scope_a = contextlib.ExitStack()
        self.xnT = self.scope_a.enter_context(st("xnT", [128, c.C, c.T], BF16))
        self.wTv = self.scope_a.enter_context(st("wTv", [128, c.C, 512], BF16))
        self.eb = self.scope_a.enter_context(st("eb", [128, 2, max(c.T, c.D // 2)], BF16))
        self.wstripA = self.scope_a.enter_context(st("wstripA", [128, 2, c.D], F32))
        self.wbfA = self.scope_a.enter_context(st("wbfA", [128, 2, c.D], BF16))
        self.wTA = self.scope_a.enter_context(st("wTA", [128, 2, c.C, 128], BF16))

        self.ps_a = contextlib.ExitStack()
        pt = nc.psum_tensor
        self.tbA = [self.ps_a.enter_context(pt(f"tbA{i}", [128, 1024], BF16)) for i in range(2)]
        self.pkA = [self.ps_a.enter_context(pt(f"pkA{i}", [128, 512], F32)) for i in range(2)]

    def alloc_rest(self):
        nc, c = self.nc, self.cfg
        st, pt = nc.sbuf_tensor, nc.psum_tensor
        self.scope_a.close()
        self.ps_a.close()
        self.scope_b = contextlib.ExitStack()
        r = self.scope_b.enter_context
        self.KH = r(st("KH", [128, 2, c.R * c.T], BF16))
        self.VH = r(st("VH", [128, 2, c.KT, 128], BF16))
        self.gbuf = r(st("gbuf", [128, 6, 1024], F32))
        self.PT = r(st("PT", [128, 6, c.G * c.QCW], BF16))
        self.Zacc = [r(st(f"Zacc{p_}", [128, 2 * c.QCW], BF16)) for p_ in range(2)]
        self.rz = r(st("rz", [128, c.QCW], F32))
        self.rzh = r(st("rzh", [128, c.QCW], BF16))
        self.ps_b = contextlib.ExitStack()
        self.Sp = [self.ps_b.enter_context(pt(f"Sp{i}", [128, c.G * c.QCW], F32)) for i in range(2)]
        self.Op = [self.ps_b.enter_context(pt(f"Op{i}", [128, c.QCW], F32)) for i in range(2)]
        self.ZB = [self.ps_b.enter_context(pt(f"ZB{i}", [128, 512], F32)) for i in range(2)]
        self.scope_b.close()
        self.ps_b.close()
        self.scope_c = contextlib.ExitStack()
        self.wTo = self.scope_c.enter_context(st("wTo", [128, c.C, c.D], BF16))
        self.x1f = self.scope_c.enter_context(st("x1f", [128, c.D], F32))
        self.x1c = self.sgnb
        self.wstripC = self.scope_c.enter_context(st("wstripC", [128, 2, c.D], F32))
        self.wbfC = self.scope_c.enter_context(st("wbfC", [128, 2, c.D], BF16))
        self.wTC = self.scope_c.enter_context(st("wTC", [128, 2, c.C, 128], BF16))
        self.ps_c = contextlib.ExitStack()
        self.tbC = [self.ps_c.enter_context(pt(f"tbC{i}", [128, 1024], BF16)) for i in range(2)]
        self.poC = self.ps_c.enter_context(pt("poC", [128, c.D], F32))
        self.scope_c.close()
        self.scope_qt.close()
        self.ps_c.close()
        self.scope_e = contextlib.ExitStack()
        r = self.scope_e.enter_context
        self.xn2T = r(st("xn2T", [128, c.C, c.T], BF16))
        self.w3f = r(st("w3f", [128, c.HF], F32))
        self.q3s = r(st("q3s", [128, c.HF], BF16))
        self.q3stage = r(st("q3stage", [128, c.HT, 128], BF16))
        self.wstrip2 = r(st("wstrip2", [128, 2, c.D], F32))
        self.qbf = r(st("qbf", [128, 2, c.D], BF16))
        self.q1T = r(st("q1T", [128, 2, c.C, 128], BF16))
        self.q2T = r(st("q2T", [128, 2, c.C, 128], BF16))
        self.geb = r(st("geb", [128, c.HF], BF16))
        self.sig = r(st("sig", [128, 2, 512], F32))
        self.hidb = r(st("hidb", [128, 2, 512], BF16))
        self.ps_d = contextlib.ExitStack()
        p = self.ps_d.enter_context
        self.tbD = [p(pt(f"tbD{i}", [128, 1024], BF16)) for i in range(2)]
        self.pfD = [p(pt(f"pfD{i}", [128, 512], F32)) for i in range(4)]
        self.scope_e.close()
        self.ps_d.close()
        self.scope_f = contextlib.ExitStack()
        r = self.scope_f.enter_context
        self.hid = r(st("hid", [128, 2, c.HT, c.SCW], BF16))
        self.q3rd = r(st("q3rd", [128, 2, c.HT, 128], BF16))
        self.outsb = r(st("outsb", [128, c.SCW // 128, c.D], F32))
        self.panel = r(st("panel", [128, 2, 512], BF16))
        self.ps_e = contextlib.ExitStack()
        p = self.ps_e.enter_context
        self.accE = [p(pt(f"accE{i}", [128, 512], F32)) for i in range(2)]
        self.tpzE = [p(pt(f"tpzE{i}", [128, 512], BF16)) for i in range(2)]

    def close_all(self):
        self.scope_f.close()
        self.ps_e.close()
        self.root.close()

    def _lp(self, f):
        def g(e):
            with self.nc.allow_low_precision(reason="fp16 softmax partial sums (error ~3e-4 rel, inside tolerance)"):
                return f(e)
        return g

    def _replica_groups(self):
        c = self.cfg
        return [list(range(b * c.R, (b + 1) * c.R)) for b in range(c.B)]

    def emit(self, w):
        self._tp_tags = {}
        self._pt_pv = None
        self._pt_z = None
        self.tb, self.pk = self.tbA, self.pkA
        self.wstrip, self.wbf, self.wT = self.wstripA, self.wbfA, self.wTA
        self.ph_ln1(w)
        self.ph_proj(w)
        w.barrier()
        self.ph_attn(w)
        w.barrier()
        self.tb = self.tbC
        self.wstrip, self.wbf, self.wT = self.wstripC, self.wbfC, self.wTC
        self.ph_outproj(w)
        w.barrier()
        self.tb = self.tbD
        self.ph_ffn1(w)
        w.barrier()
        self.ph_ffn2(w)

    def ln_tile(self, w, xin_f, out_bf, out_war=None, idx=0, disc=None, eps=1e-5):
        c = self.cfg
        D = c.D
        s = self.lns[idx]
        sqb = disc
        w.ins("scalar", lambda e, sqb=sqb: e.activation(out=sqb, in_=xin_f[:, 0:c.D // 2],
                                               func=AF.Square, accum_out=s["ssq"]), inc="S")
        w.wait("scalar", "S", w.snap("S"))
        w.ins("scalar", lambda e, sqb=sqb: e.activation(out=sqb, in_=xin_f[:, c.D // 2:c.D],
                                               func=AF.Square, accum_out=s["m2"]), inc="S")
        sq_done = w.snap("S")
        w.ins("vector", lambda e: e.tensor_reduce(out=s["sx"], in_=xin_f,
                                                  axis=AX.X, op=ALU.add), inc="V")
        w.wait("vector", "V", w.snap("V"))
        w.ins("vector", lambda e: e.tensor_scalar(out=s["mn"], in0=s["sx"],
                                                  scalar1=1.0 / D, scalar2=None, op0=ALU.mult), inc="V")
        w.wait("vector", "S", sq_done)
        w.wait("vector", "V", w.snap("V"))
        w.ins("vector", lambda e: e.tensor_tensor(out=s["ssq"], in0=s["ssq"],
                                                  in1=s["m2"], op=ALU.add), inc="V")
        w.wait("vector", "V", w.snap("V"))
        w.ins("vector", lambda e: e.tensor_tensor(out=s["m2"], in0=s["mn"],
                                                  in1=s["mn"], op=ALU.mult), inc="V")
        w.wait("vector", "V", w.snap("V"))
        w.ins("vector", lambda e: e.scalar_tensor_tensor(out=s["var"], in0=s["ssq"],
                                                         scalar=1.0 / D, in1=s["m2"],
                                                         op0=ALU.mult, op1=ALU.subtract), inc="V")
        v_var = w.snap("V")
        w.wait("scalar", "V", v_var)
        w.ins("scalar", lambda e: e.activation(out=s["var"], in_=s["var"],
                                               func=AF.Sqrt, bias=self.cst[:, 0:1]), inc="S")
        s_sqrt = w.snap("S")
        w.wait("vector", "S", s_sqrt)
        w.ins("vector", lambda e: e.reciprocal(out=s["rstd"], in_=s["var"]), inc="V")
        w.wait("vector", "V", w.snap("V"))
        if out_war is not None:
            w.wait("vector", out_war[0], out_war[1])
        w.ins("vector", lambda e: e.tensor_scalar(out=out_bf, in0=xin_f,
                                                  scalar1=s["mn"], scalar2=s["rstd"],
                                                  op0=ALU.subtract, op1=ALU.mult), inc="V")
        return w.snap("V")

    def transpose_tile_to(self, w, src_bf, dst, st_idx, wait_v=None):
        c = self.cfg
        for half in range(c.C // 8):
            bank = self.tb[half % 2]
            tag = self._tp_tags.get(("tb", half % 2))
            if tag is not None:
                w.wait("tensor", "V", tag)
            if wait_v is not None:
                w.wait("tensor", "V", wait_v)
                wait_v = None
            for j in range(8):
                cc_ = half * 8 + j
                w.ins("tensor", lambda e, cc_=cc_, j=j, bank=bank: e.transpose(
                    bank[:, j * 128:(j + 1) * 128],
                    src_bf[:, cc_ * 128:(cc_ + 1) * 128],
                    self.id_b[:, :]),
                    inc="T" if j == 7 else None)
            t_done = w.snap("T")
            w.wait("vector", "T", t_done)
            w.ins("vector", lambda e, half=half, bank=bank: e.tensor_copy(
                dst[:, half * 8:(half + 1) * 8, st_idx * 128:(st_idx + 1) * 128],
                bank[:, :].rearrange("p (j f) -> p j f", f=128)), inc="V")
            self._tp_tags[("tb", half % 2)] = w.snap("V")
        return w.snap("T")

    def wstrip_prep(self, w, src_dram, row0, slot, dst=None):
        c = self.cfg
        tag = self._tp_tags.get(("wstrip", slot))
        if tag is not None:
            w.wait("sync", "V", tag)
        lsem = "ldw0" if slot == 0 else "ldw1"
        w.ins("sync", lambda e: e.dma_start(out=self.wstrip[:, slot, :],
                                            in_=src_dram[row0:row0 + 128, :]), inc=lsem, by=16)
        ld = w.snap(lsem)
        w.wait("vector", lsem, ld)
        w.ins("vector", lambda e: e.tensor_copy(self.wbf[:, slot, :], self.wstrip[:, slot, :]), inc="V")
        self._tp_tags[("wstrip", slot)] = w.snap("V")
        cv = w.snap("V")
        for half in range(c.C // 8):
            bank = self.tb[half % 2]
            btag = self._tp_tags.get(("tb", half % 2))
            if btag is not None:
                w.wait("tensor", "V", btag)
            if half == 0:
                w.wait("tensor", "V", cv)
            for j in range(8):
                cc_ = half * 8 + j
                w.ins("tensor", lambda e, cc_=cc_, j=j, bank=bank: e.transpose(
                    bank[:, j * 128:(j + 1) * 128],
                    self.wbf[:, slot, cc_ * 128:(cc_ + 1) * 128],
                    self.id_b[:, :]),
                    inc="T" if j == 7 else None)
            td = w.snap("T")
            w.wait("vector", "T", td)
            if dst is None:
                w.ins("vector", lambda e, half=half, bank=bank: e.tensor_copy(
                    self.wT[:, slot, half * 8:(half + 1) * 8, :],
                    bank[:, :].rearrange("p (j f) -> p j f", f=128)), inc="V")
            else:
                w.ins("vector", lambda e, half=half, bank=bank: e.tensor_copy(
                    dst(half), bank[:, :].rearrange("p (j f) -> p j f", f=128)), inc="V")
            self._tp_tags[("tb", half % 2)] = w.snap("V")
        return w.snap("V")

    def ph_ln1(self, w):
        c = self.cfg
        w.ins("sync", lambda e: e.dma_start(out=self.id_b[:, :], in_=self.ident[:, :]), inc="ldpro", by=16)
        w.ins("sync", lambda e: e.dma_start(out=self.ones_f[:, :], in_=self.ones[:, :]), inc="ldpro", by=16)
        w.ins("sync", lambda e: e.dma_start(out=self.cst[:, :], in_=self.consts[:, :]), inc="ldpro", by=16)
        ld0 = w.snap("ldpro")
        for eng in ("vector", "tensor", "scalar", "gpsimd"):
            w.wait(eng, "ldpro", ld0)
        w.ins("vector", lambda e: e.tensor_copy(self.ones_b[:, :], self.ones_f[:, :]), inc="V")
        self.v_onesb = w.snap("V")
        w.wait("tensor", "V", self.v_onesb)
        for stt in range(c.NT):
            slot = stt % 2
            tag = self._tp_tags.get(("xstrip", slot))
            if tag is not None:
                w.wait("gpsimd", "V", tag)
            xsem = "ldx0" if slot == 0 else "ldx1"
            w.ins("gpsimd", lambda e, stt=stt, slot=slot: e.dma_start(
                out=self.xstrip[:, slot, :],
                in_=self.x[stt * 128:(stt + 1) * 128, :]), inc=xsem, by=16)
            ld = w.snap(xsem)
            w.wait("vector", xsem, ld)
            w.wait("scalar", xsem, ld)
            nb_war = self._tp_tags.get(("nb", slot))
            vdone = self.ln_tile(w, self.xstrip[:, slot, :], self.nb[:, slot, :],
                                 out_war=("T", nb_war) if nb_war else None, idx=slot,
                                 disc=self.eb[:, slot, 0:c.D // 2])
            self._tp_tags[("xstrip", slot)] = vdone
            t_tp = self.transpose_tile_to(w, self.nb[:, slot, :], self.xnT, stt, wait_v=vdone)
            self._tp_tags[("nb", slot)] = t_tp
        self.v_xnT_done = w.snap("V")
        if c.dbg:
            w.wait("gpsimd", "V", self.v_xnT_done)
            w.ins("gpsimd", lambda e: e.dma_start(
                out=self.dbg_xnT[:, :].rearrange("p (a b) -> p a b", b=c.T),
                in_=self.xnT[:, :, :]), inc="stc", by=16)
            w.wait("gpsimd", "stc", w.snap("stc"))

    def ph_proj(self, w):
        c = self.cfg
        scw = min(512, c.T)
        nsc = c.T // scw

        for et in range(c.E):
            slot = et % 2
            rdy = self.wstrip_prep(w, self.wk, et * 128, slot)
            for sc in range(nsc):
                pbank = self.pk[sc % 2]
                ptag = self._tp_tags.get(("pk", sc % 2))
                if ptag is not None:
                    w.wait("tensor", "V", ptag)
                if sc == 0:
                    w.wait("tensor", "V", max(rdy, self.v_xnT_done))
                for cc_ in range(c.C):
                    w.ins("tensor", lambda e, cc_=cc_, sc=sc, slot=slot, pbank=pbank: e.matmul(
                        pbank[:, 0:scw],
                        self.wT[:, slot, cc_, :],
                        self.xnT[:, cc_, sc * scw:(sc + 1) * scw],
                        start=(cc_ == 0), stop=(cc_ == c.C - 1)),
                        inc="T" if cc_ == c.C - 1 else None)
                mm = w.snap("T")
                w.wait("vector", "T", mm)
                etag = self._tp_tags.get(("eb", slot))
                if etag is not None:
                    w.wait("vector", "stc", w.snap("stc"))
                w.ins("vector", lambda e, sc=sc, slot=slot, pbank=pbank: e.tensor_copy(
                    self.eb[:, slot, sc * scw:(sc + 1) * scw], pbank[:, 0:scw]), inc="V")
                self._tp_tags[("pk", sc % 2)] = w.snap("V")
            row_done = w.snap("V")
            w.wait("gpsimd", "V", row_done)
            w.ins("gpsimd", lambda e, et=et, slot=slot: e.dma_start(
                out=self.cc_in_k[et * 128:(et + 1) * 128, :],
                in_=self.eb[:, slot, 0:c.T]), inc="stc", by=16)
            self._tp_tags[("eb", slot)] = w.snap("stc")
        st_k = w.snap("stc")
        w.wait("gpsimd", "stc", w.snap("stc"))
        w.ins("gpsimd", lambda e: e.collective_compute(
            "AllGather", ALU.bypass,
            replica_groups=self._replica_groups(),
            ins=[self.cc_in_k.ap().opt()], outs=[self.cc_out_k.ap().opt()]), inc="ccK")

        for ec in range(c.D // 512):
            for j in range(4):
                et = ec * 4 + j
                slot = et % 2
                wtv_tag = self._tp_tags.get("wTv")
                if wtv_tag is not None and j == 0:
                    w.wait("vector", "T", wtv_tag)
                self.wstrip_prep(w, self.wv, et * 128, slot,
                                 dst=lambda half, j=j: self.wTv[:, half * 8:(half + 1) * 8,
                                                               j * 128:(j + 1) * 128])
            wtv_ready = w.snap("V")
            for stt in range(c.NT):
                pbank = self.pk[stt % 2]
                ptag = self._tp_tags.get(("pk", stt % 2))
                if ptag is not None:
                    w.wait("tensor", "V", ptag)
                if stt == 0:
                    w.wait("tensor", "V", wtv_ready)
                for cc_ in range(c.C):
                    w.ins("tensor", lambda e, cc_=cc_, stt=stt, pbank=pbank: e.matmul(
                        pbank[:, :],
                        self.xnT[:, cc_, stt * 128:(stt + 1) * 128],
                        self.wTv[:, cc_, :],
                        start=(cc_ == 0), stop=(cc_ == c.C - 1)),
                        inc="T" if cc_ == c.C - 1 else None)
                mm = w.snap("T")
                self._tp_tags["wTv"] = mm
                w.wait("vector", "T", mm)
                eslot = stt % 2
                etag = self._tp_tags.get(("eb", eslot))
                if etag is not None:
                    w.wait("vector", "stc", w.snap("stc"))
                w.ins("vector", lambda e, eslot=eslot, pbank=pbank: e.tensor_copy(
                    self.eb[:, eslot, 0:512], pbank[:, :]), inc="V")
                vdone = w.snap("V")
                self._tp_tags[("pk", stt % 2)] = vdone
                w.wait("gpsimd", "V", vdone)
                w.ins("gpsimd", lambda e, stt=stt, ec=ec, eslot=eslot: e.dma_start(
                    out=self.cc_in_v[stt * 128:(stt + 1) * 128, ec * 512:(ec + 1) * 512],
                    in_=self.eb[:, eslot, 0:512]), inc="stc", by=16)
                self._tp_tags[("eb", eslot)] = w.snap("stc")
        st_v = w.snap("stc")
        w.wait("gpsimd", "stc", w.snap("stc"))
        w.ins("gpsimd", lambda e: e.collective_compute(
            "AllGather", ALU.bypass,
            replica_groups=self._replica_groups(),
            ins=[self.cc_in_v.ap().opt()], outs=[self.cc_out_v.ap().opt()]), inc="ccV")

        for et in range(c.E):
            slot = et % 2
            rdy = self.wstrip_prep(w, self.wq, et * 128, slot)
            for sc in range(nsc):
                pbank = self.pk[sc % 2]
                ptag = self._tp_tags.get(("pk", sc % 2))
                if ptag is not None:
                    w.wait("tensor", "V", ptag)
                if sc == 0:
                    w.wait("tensor", "V", rdy)
                for cc_ in range(c.C):
                    w.ins("tensor", lambda e, cc_=cc_, sc=sc, slot=slot, pbank=pbank: e.matmul(
                        pbank[:, 0:scw],
                        self.wT[:, slot, cc_, :],
                        self.xnT[:, cc_, sc * scw:(sc + 1) * scw],
                        start=(cc_ == 0), stop=(cc_ == c.C - 1)),
                        inc="T" if cc_ == c.C - 1 else None)
                mm = w.snap("T")
                w.wait("vector", "T", mm)
                w.ins("vector", lambda e, et=et, sc=sc, pbank=pbank: e.tensor_scalar(
                    out=self.QT[:, et, sc * scw:(sc + 1) * scw], in0=pbank[:, 0:scw],
                    scalar1=float(c.SCALE), scalar2=None, op0=ALU.mult), inc="V")
                self._tp_tags[("pk", sc % 2)] = w.snap("V")
        if c.dbg:
            v_q = w.snap("V")
            w.wait("gpsimd", "V", v_q)
            w.ins("gpsimd", lambda e: e.dma_start(
                out=self.dbg_qT[:, :].rearrange("p (a b) -> p a b", b=c.T),
                in_=self.QT[:, :, :]), inc="stc", by=16)
            w.wait("gpsimd", "stc", w.snap("stc"))

    def ph_attn(self, w):
        c = self.cfg
        jobs = self._gamma_jobs()
        n_iter = c.H * c.NQC
        per = min(-(-len(jobs) // n_iter), 6)
        job_idx = [0]
        gam_first = [True, True, True]
        gam_firstV = [True, True, True]
        gslot_tag = {}

        last_exp = 0
        evac_snap = {}
        zb_free = 0
        prev_zmm = [0, 0]
        pt_user = {}
        pending_fin = [None]

        def fin_stage1(w):
            st_ = pending_fin[0]
            if st_ is None or st_.get("stage") != 0:
                return
            (fh, fqc, fgi, fob, fz_last, fpar) = st_["t"]
            zb = self.ZB[fpar]
            w.wait("tensor", "V", fz_last["vector"])
            w.wait("tensor", "V", zb_free[fpar])
            for i in range(2):
                w.ins("tensor", lambda e, i=i, fpar=fpar, zb=zb: e.matmul(
                    zb[0:1, 0:c.QCW], self.ones_b[:, 0:1],
                    self.Zacc[fpar][:, i * c.QCW:(i + 1) * c.QCW],
                    start=(i == 0), stop=(i == 1), skip_group_check=True),
                    inc="T" if i == 1 else None)
            z_mm = w.snap("T")
            prev_zmm[fpar] = z_mm
            w.wait("vector", "T", z_mm)
            w.wait("vector", "T", last_bcast[0])
            w.ins("vector", self._lp(lambda e, zb=zb: e.reciprocal(out=self.rzh[0:1, :], in_=zb[0:1, 0:c.QCW])), inc="V")
            st_["v_recip"] = w.snap("V")
            st_["stage"] = 1

        def fin_stage2(w):
            st_ = pending_fin[0]
            if st_ is None or st_.get("stage") != 1:
                return
            pending_fin[0] = None
            (fh, fqc, fgi, fob, fz_last, fpar) = st_["t"]
            zb = self.ZB[fpar]
            w.wait("tensor", "V", st_["v_recip"])
            w.ins("tensor", lambda e, zb=zb: e.matmul(
                zb[:, 0:c.QCW], self.ones_b[0:1, :], self.rzh[0:1, :],
                start=True, stop=True, skip_group_check=True), inc="T")
            bc_mm = w.snap("T")
            last_bcast[0] = bc_mm
            w.wait("vector", "T", bc_mm)
            w.wait("vector", "V", last_evac[0])
            w.ins("vector", lambda e, zb=zb: e.tensor_copy(self.rz[:, :], zb[:, 0:c.QCW]), inc="V")
            zb_free[fpar] = w.snap("V")
            w.wait("vector", "T", fz_last["pv_done"])
            w.wait("vector", "V", zb_free[fpar])
            w.ins("vector", lambda e, fh=fh, fqc=fqc, fob=fob: e.tensor_tensor(
                out=self.QT[:, fh, fqc * c.QCW:(fqc + 1) * c.QCW],
                in0=fob[:, :], in1=self.rz[:, :], op=ALU.mult), inc="V")
            evac_snap[("o", fgi % 2, fgi)] = w.snap("V")
            last_evac[0] = w.snap("V")

        zb_free = [0, 0]
        last_evac = [0]
        last_bcast = [0]
        exp_of_slot = [0, 0]
        for h in range(c.H):
            hb = h % 2
            ktag = self._tp_tags.get(("kh", hb))
            if ktag is not None:
                w.wait("sync", "T", ktag)
            if h == 0:
                w.wait("sync", "ccK", 1)
            ksem = "ldk0" if hb == 0 else "ldk1"
            vsem = "ldv0" if hb == 0 else "ldv1"
            for r_ in range(c.R):
                w.ins("sync", lambda e, h=h, hb=hb, r_=r_: e.dma_start(
                    out=self.KH[:, hb, r_ * c.T:(r_ + 1) * c.T],
                    in_=self.cc_out_k[r_ * c.D + h * 128: r_ * c.D + (h + 1) * 128, :]),
                    inc=ksem, by=16)
            if h == 0:
                w.wait("sync", "ccV", 1)
            ktpr = c.KT // c.R
            for r_ in range(c.R):
                w.ins("sync", lambda e, h=h, hb=hb, r_=r_, ktpr=ktpr: e.dma_start(
                    out=self.VH[:, hb, r_ * ktpr:(r_ + 1) * ktpr, :],
                    in_=self.cc_out_v[r_ * c.T:(r_ + 1) * c.T, h * 128:(h + 1) * 128]
                        .rearrange("(t p) e -> p t e", p=128)),
                    inc=vsem, by=16)
            ldk_h = w.snap(ksem)
            ldv_h = w.snap(vsem)

            for qc in range(c.NQC):
                gi_ = h * c.NQC + qc
                par = gi_ % 2
                ob = self.Op[gi_ % 2]
                my_jobs = []
                for k_ in range(per):
                    if job_idx[0] < len(jobs):
                        my_jobs.append((job_idx[0], jobs[job_idx[0]]))
                        job_idx[0] += 1
                for ji, (gidx, apf, width) in my_jobs:
                    slot = ji % 6
                    gsem = f"ldg{slot}"
                    tag = gslot_tag.get(slot)
                    if tag is not None:
                        w.wait("gpsimd", tag[0], tag[1])
                    w.ins("gpsimd", lambda e, apf=apf, slot=slot, width=width: e.dma_start(
                        out=self.gbuf[:, slot, 0:width], in_=apf()), inc=gsem, by=16)

                def gamma_one(ji, gidx, width):
                    slot = ji % 6
                    gsem = f"ldg{slot}"
                    if ji % 2 == 0:
                        geng, gctr = "vector", "V"
                    else:
                        geng, gctr = "scalar", "S"
                    w.wait(geng, gsem, (ji // 6 + 1) * 16)
                    w.wait(geng, gctr, w.snap(gctr))
                    if geng == "scalar":
                        w.ins("scalar", lambda e, slot=slot, width=width: e.activation(
                            out=self.sqb[:, 0:width], in_=self.gbuf[:, slot, 0:width],
                            func=AF.Abs, accum_out=self.gtmp), inc="S")
                    else:
                        w.ins("vector", lambda e, slot=slot, width=width: e.tensor_reduce(
                            out=self.gtmpV, in_=self.gbuf[:, slot, 0:width],
                            axis=AX.X, op=ALU.add, apply_absolute_value=True), inc="V")
                    gslot_tag[slot] = (gctr, w.snap(gctr))
                    w.wait(geng, gctr, w.snap(gctr))
                    gt = self.gtmp if geng == "scalar" else self.gtmpV
                    ga = self.gacc if geng == "scalar" else self.gaccV
                    gfirst = gam_first if geng == "scalar" else gam_firstV
                    if gfirst[gidx]:
                        if geng == "scalar":
                            w.ins("scalar", lambda e, gidx=gidx, gt=gt, ga=ga: e.copy(ga[gidx], gt), inc="S")
                        else:
                            w.ins("vector", lambda e, gidx=gidx, gt=gt, ga=ga: e.tensor_copy(ga[gidx], gt), inc="V")
                        gfirst[gidx] = False
                    else:
                        if geng == "scalar":
                            w.ins("scalar", lambda e, gidx=gidx, gt=gt, ga=ga: e.add(ga[gidx], ga[gidx], gt), inc="S")
                        else:
                            w.ins("vector", lambda e, gidx=gidx, gt=gt, ga=ga: e.tensor_tensor(
                                out=ga[gidx], in0=ga[gidx], in1=gt, op=ALU.add), inc="V")

                first_z = {"vector": True}
                z_last = {"vector": None, "pv_done": None}
                exp_snap = {}
                pt_z_of = {}
                for g in range(c.NG + 1):
                    if g < c.NG:
                        pb = g % 6
                        sslot = g % 2
                        sp = self.Sp[sslot]
                        w.wait("tensor", "S", exp_of_slot[sslot])
                        if g == 0 and qc == 0:
                            w.wait("tensor", ksem, ldk_h)
                        if g == 0:
                            prev = evac_snap.get(("o", gi_ % 2, gi_ - 2))
                            if prev is not None:
                                w.wait("tensor", "V", prev)
                        for i in range(c.G):
                            kt = g * c.G + i
                            w.ins("tensor", lambda e, kt=kt, i=i, qc=qc, hb=hb, h=h, sp=sp: e.matmul(
                                sp[:, i * c.QCW:(i + 1) * c.QCW],
                                self.KH[:, hb, kt * 128:(kt + 1) * 128],
                                self.QT[:, h, qc * c.QCW:(qc + 1) * c.QCW],
                                start=True, stop=True, skip_group_check=True),
                                inc="T" if i == c.G - 1 else None)
                        s_mm = w.snap("T")
                        pu = pt_user.get(pb)
                        tw = s_mm if pu is None else max(s_mm, pu[0])
                        w.wait("scalar", "T", tw)
                        if pu is not None:
                            w.wait("scalar", pu[1], pu[2])
                        w.ins("scalar", lambda e, pb=pb, sp=sp: e.activation(
                            out=self.PT[:, pb, :], in_=sp[:, :], func=AF.Exp), inc="S")
                        last_exp = w.snap("S")
                        exp_of_slot[sslot] = last_exp
                        zeng = "vector"
                        zctr = "V"
                        acc = self.Zacc[par]
                        w.wait(zeng, "S", last_exp)
                        if z_last[zeng] is not None:
                            w.wait(zeng, zctr, z_last[zeng])
                        if first_z[zeng]:
                            w.wait(zeng, "T", prev_zmm[par])
                            w.ins(zeng, lambda e, pb=pb, acc=acc: e.tensor_copy(
                                acc[:, :], self.PT[:, pb, :]), inc=zctr)
                            first_z[zeng] = False
                        else:
                            w.ins(zeng, self._lp(lambda e, pb=pb, acc=acc: e.tensor_tensor(
                                out=acc[:, :], in0=acc[:, :],
                                in1=self.PT[:, pb, :], op=ALU.add)), inc=zctr)
                        z_last[zeng] = w.snap(zctr)
                        pt_z = (zctr, z_last[zeng])
                    if g < c.NG:
                        for _ji in range(len(my_jobs)):
                            if g == min(3 + 6 * _ji, c.NG - 1):
                                gamma_one(my_jobs[_ji][0], my_jobs[_ji][1][0], my_jobs[_ji][1][2])
                    if g == min(1, c.NG - 1):
                        fin_stage1(w)
                    if g == min(8, c.NG):
                        fin_stage2(w)
                    gp = g - 1
                    if gp >= 0:
                        pbp = gp % 6
                        w.wait("tensor", "S", exp_snap[gp])
                        if gp == 0 and qc == 0:
                            w.wait("tensor", vsem, ldv_h)
                        for i in range(c.G):
                            kt = gp * c.G + i
                            w.ins("tensor", lambda e, kt=kt, i=i, hb=hb, ob=ob, pbp=pbp: e.matmul(
                                ob[:, :],
                                self.VH[:, hb, kt, :],
                                self.PT[:, pbp, i * c.QCW:(i + 1) * c.QCW],
                                start=(kt == 0), stop=(kt == c.KT - 1),
                                skip_group_check=True),
                                inc="T" if i == c.G - 1 else None)
                        pt_user[pbp] = (w.snap("T"), pt_z_of[gp][0], pt_z_of[gp][1])
                    if g < c.NG:
                        exp_snap[g] = last_exp
                        pt_z_of[g] = pt_z
                z_last["pv_done"] = w.snap("T")
                pending_fin[0] = {"t": (h, qc, gi_, ob, dict(z_last), par), "stage": 0}
            self._tp_tags[("kh", hb)] = w.snap("T")
        fin_stage1(w)
        fin_stage2(w)

        while job_idx[0] < len(jobs):
            ji = job_idx[0]
            gidx, apf, width = jobs[job_idx[0]]
            job_idx[0] += 1
            slot = ji % 6
            gsem = f"ldg{slot}"
            tag = gslot_tag.get(slot)
            if tag is not None:
                w.wait("gpsimd", tag[0], tag[1])
            w.ins("gpsimd", lambda e, apf=apf, slot=slot, width=width: e.dma_start(
                out=self.gbuf[:, slot, 0:width], in_=apf()), inc=gsem, by=16)
            w.wait("scalar", gsem, (ji // 6 + 1) * 16)
            w.wait("scalar", "S", w.snap("S"))
            w.ins("scalar", lambda e, slot=slot, width=width: e.activation(
                out=self.sqb[:, 0:width], in_=self.gbuf[:, slot, 0:width],
                func=AF.Abs, accum_out=self.gtmp), inc="S")
            gslot_tag[slot] = ("S", w.snap("S"))
            w.wait("scalar", "S", w.snap("S"))
            if gam_first[gidx]:
                w.ins("scalar", lambda e, gidx=gidx: e.copy(self.gacc[gidx], self.gtmp), inc="S")
                gam_first[gidx] = False
            else:
                w.ins("scalar", lambda e, gidx=gidx: e.add(
                    self.gacc[gidx], self.gacc[gidx], self.gtmp), inc="S")
        self.g_done = w.snap("S")

        self.emit_gamma_finalize(w)
        if c.dbg:
            v_all = w.snap("V")
            w.wait("gpsimd", "V", v_all)
            w.ins("gpsimd", lambda e: e.dma_start(
                out=self.dbg_att[:, :].rearrange("p (a b) -> p a b", b=c.T),
                in_=self.QT[:, :, :]), inc="stc", by=16)
            w.wait("gpsimd", "stc", w.snap("stc"))
            for gi in range(3):
                w.ins("gpsimd", lambda e, gi=gi: e.dma_start(
                    out=self.dbg_thr[:, gi:gi + 1], in_=self.thr[gi],
                    allow_slow_non_contiguous=True), inc="stc", by=16)
            w.wait("gpsimd", "stc", w.snap("stc"))

    def _gamma_jobs(self):
        c = self.cfg
        jobs = []
        nw = c.D // 1024
        for mi, mat in enumerate([self.w1, self.w2]):
            for j in range(c.HF // 128):
                for k in range(nw):
                    jobs.append((mi, lambda mat=mat, j=j, k=k: mat[j * 128:(j + 1) * 128, k * 1024:(k + 1) * 1024], 1024))
        hq = c.HF // 8
        assert hq <= 1024
        for j in range(c.D // 128):
            for k in range(8):
                jobs.append((2, lambda j=j, k=k: self.w3[j * 128:(j + 1) * 128, k * hq:(k + 1) * hq], hq))
        return jobs

    def emit_gamma_finalize(self, w):
        c = self.cfg
        n_el = float(c.HF * c.D)
        for gi in range(3):
            w.wait("vector", "S", self.g_done)
            w.wait("vector", "V", w.snap("V"))
            w.ins("vector", lambda e, gi=gi: e.tensor_tensor(
                out=self.gaccV[gi], in0=self.gaccV[gi], in1=self.gacc[gi], op=ALU.add), inc="V")
            w.wait("tensor", "V", w.snap("V"))
            w.ins("tensor", lambda e, gi=gi: e.matmul(
                self.ZB[0][0:1, 0:1], self.ones_f[:, 0:1], self.gaccV[gi],
                start=True, stop=True, skip_group_check=True), inc="T")
            t0 = w.snap("T")
            w.wait("scalar", "T", t0)
            w.wait("scalar", "S", w.snap("S"))
            w.ins("scalar", lambda e, gi=gi: e.mul(self.std[0:1, :], self.ZB[0][0:1, 0:1], 0.5 / n_el), inc="S")
            w.wait("scalar", "S", w.snap("S"))
            w.ins("scalar", lambda e: e.add(self.std[0:1, :], self.std[0:1, :], self.cst[0:1, 1:2]), inc="S")
            s1 = w.snap("S")
            w.wait("tensor", "S", s1)
            w.wait("tensor", "V", w.snap("V"))
            w.ins("tensor", lambda e: e.matmul(
                self.ZB[0][:, 0:1], self.ones_f[0:1, :], self.std[0:1, :],
                start=True, stop=True, skip_group_check=True), inc="T")
            t1 = w.snap("T")
            w.wait("vector", "T", t1)
            w.ins("vector", lambda e, gi=gi: e.tensor_copy(self.thr[gi], self.ZB[0][:, 0:1]), inc="V")
            w.wait("vector", "V", w.snap("V"))
            w.ins("vector", lambda e, gi=gi: e.tensor_scalar(
                out=self.nthr[gi], in0=self.thr[gi],
                scalar1=-1.0, scalar2=None, op0=ALU.mult), inc="V")

    def ph_outproj(self, w):
        c = self.cfg
        wto_ec_ready = []
        for et in range(c.E):
            slot = et % 2
            self.wstrip_prep(w, self.wo, et * 128, slot,
                             dst=lambda half, et=et: self.wTo[:, half * 8:(half + 1) * 8,
                                                             et * 128:(et + 1) * 128])
            if et % 4 == 3:
                wto_ec_ready.append(w.snap("V"))

        ev_tag = 0
        for stt in range(c.NT):
            slot = stt % 2
            tag = self._tp_tags.get(("xstrip", slot))
            if tag is not None:
                w.wait("gpsimd", "V", tag)
            xsem = "ldx0" if slot == 0 else "ldx1"
            w.ins("gpsimd", lambda e, stt=stt, slot=slot: e.dma_start(
                out=self.xstrip[:, slot, :],
                in_=self.x[stt * 128:(stt + 1) * 128, :]), inc=xsem, by=16)
            ld = w.snap(xsem)
            if stt > 0:
                w.wait("tensor", "V", ev_tag)
            for ec in range(c.D // 512):
                if stt == 0:
                    w.wait("tensor", "V", wto_ec_ready[ec])
                for cc_ in range(c.C):
                    w.ins("tensor", lambda e, cc_=cc_, stt=stt, ec=ec: e.matmul(
                        self.poC[:, ec * 512:(ec + 1) * 512],
                        self.QT[:, cc_, stt * 128:(stt + 1) * 128],
                        self.wTo[:, cc_, ec * 512:(ec + 1) * 512],
                        start=(cc_ == 0), stop=(cc_ == c.C - 1), skip_group_check=True),
                        inc="T" if (cc_ == c.C - 1 and ec == c.D // 512 - 1) else None)
            mm = w.snap("T")
            w.wait("vector", "T", mm)
            w.wait("vector", xsem, ld)
            x1tag = self._tp_tags.get("x1f")
            if x1tag is not None:
                w.wait("vector", "V", x1tag)
            w.ins("vector", lambda e, slot=slot: e.tensor_tensor(
                out=self.x1f[:, :], in0=self.xstrip[:, slot, :],
                in1=self.poC[:, :], op=ALU.add), inc="V")
            v_x1 = w.snap("V")
            ev_tag = v_x1
            self._tp_tags[("xstrip", slot)] = v_x1
            w.wait("vector", "V", v_x1)
            w.wait("vector", "stc", w.snap("stc"))
            w.ins("vector", lambda e: e.tensor_copy(self.x1c[:, :], self.x1f[:, :]), inc="V")
            v_x1c = w.snap("V")
            w.wait("gpsimd", "V", v_x1c)
            w.ins("gpsimd", lambda e, stt=stt: e.dma_start(
                out=self.x1_dram[:, stt * c.D:(stt + 1) * c.D], in_=self.x1c[:, :]), inc="stc", by=16)
            w.wait("scalar", "V", v_x1)
            nb_war = ("stc", w.snap("stc")) if stt > 1 else None
            v_ln = self.ln_tile(w, self.x1f[:, :], self.nb[:, slot, :], out_war=nb_war, idx=slot,
                                disc=self.wbf[:, slot, 0:c.D // 2])
            self._tp_tags["x1f"] = v_ln
            w.wait("gpsimd", "V", v_ln)
            w.ins("gpsimd", lambda e, stt=stt, slot=slot: e.dma_start(
                out=self.xn2_dram[:, stt * c.D:(stt + 1) * c.D], in_=self.nb[:, slot, :]), inc="stc", by=16)

    def ph_ffn1(self, w):
        c = self.cfg
        xn2_sc_ready = []
        for stt in range(c.NT):
            slot = stt % 2
            ntag = self._tp_tags.get(("nb", slot))
            if ntag is not None:
                w.wait("sync", "T", ntag)
            hsem = "ldh0" if slot == 0 else "ldh1"
            w.ins("sync", lambda e, stt=stt, slot=slot: e.dma_start(
                out=self.nb[:, slot, :], in_=self.xn2_dram[:, stt * c.D:(stt + 1) * c.D]), inc=hsem, by=16)
            ld = w.snap(hsem)
            w.wait("tensor", hsem, ld)
            t_tp = self.transpose_tile_to(w, self.nb[:, slot, :], self.xn2T, stt, wait_v=None)
            self._tp_tags[("nb", slot)] = t_tp
            if (stt + 1) % (c.SCW // 128) == 0:
                xn2_sc_ready.append(w.snap("V"))

        w3_q_at = {}
        step = max(1, c.HT // c.E)
        for k in range(c.E):
            w3_q_at.setdefault(min(k * step, c.HT - 2), []).append(k)
        w3_pending = []

        def prep_q(ht):
            for im, (mat, gi) in enumerate([(self.w1, 0), (self.w2, 1)]):
                slot = im
                tag = self._tp_tags.get(("wstrip2", slot))
                if tag is not None:
                    w.wait("sync", "V", tag)
                lsem = "ldw0" if slot == 0 else "ldw1"
                w.ins("sync", lambda e, mat=mat, ht=ht, slot=slot: e.dma_start(
                    out=self.wstrip2[:, slot, :], in_=mat[ht * 128:(ht + 1) * 128, :]), inc=lsem, by=16)
                ld = w.snap(lsem)
                w.wait("vector", lsem, ld)
                w.wait("vector", "V", w.snap("V"))
                qtag = self._tp_tags.get(("qbf", slot))
                if qtag is not None:
                    w.wait("vector", "T", qtag)
                w.ins("vector", lambda e, slot=slot, gi=gi: e.tensor_scalar(
                    out=self.nb[:, 0, :], in0=self.wstrip2[:, slot, :],
                    scalar1=self.thr[gi], scalar2=None, op0=ALU.is_ge), inc="V")
                w.ins("vector", lambda e, slot=slot, gi=gi: e.tensor_scalar(
                    out=self.sgnb[:, :], in0=self.wstrip2[:, slot, :],
                    scalar1=self.nthr[gi], scalar2=None, op0=ALU.is_ge), inc="V")
                vb = w.snap("V")
                w.wait("vector", "V", vb)
                w.ins("vector", lambda e, slot=slot: e.scalar_tensor_tensor(
                    out=self.qbf[:, slot, :], in0=self.nb[:, 0, :], scalar=-1.0,
                    in1=self.sgnb[:, :], op0=ALU.add, op1=ALU.add), inc="V")
                self._tp_tags[("wstrip2", slot)] = w.snap("V")
            return w.snap("V")

        def prep_t(ht, q_ready):
            bslot = ht % 2
            for im, qT in enumerate([self.q1T, self.q2T]):
                slot = im
                for half in range(c.C // 8):
                    bank = self.tb[half % 2]
                    btag = self._tp_tags.get(("tb", half % 2))
                    if btag is not None:
                        w.wait("tensor", "V", btag)
                    if half == 0:
                        w.wait("tensor", "V", q_ready)
                        utag = self._tp_tags.get(("qTuse", bslot))
                        if utag is not None:
                            w.wait("tensor", "T", utag)
                    for j in range(8):
                        cc_ = half * 8 + j
                        w.ins("tensor", lambda e, cc_=cc_, j=j, bank=bank, slot=slot: e.transpose(
                            bank[:, j * 128:(j + 1) * 128],
                            self.qbf[:, slot, cc_ * 128:(cc_ + 1) * 128],
                            self.id_b[:, :]),
                            inc="T" if j == 7 else None)
                    td = w.snap("T")
                    self._tp_tags[("qbf", slot)] = td
                    w.wait("vector", "T", td)
                    w.ins("vector", lambda e, half=half, bank=bank, qT=qT, bslot=bslot: e.tensor_copy(
                        qT[:, bslot, half * 8:(half + 1) * 8, :],
                        bank[:, :].rearrange("p (j f) -> p j f", f=128)), inc="V")
                    self._tp_tags[("tb", half % 2)] = w.snap("V")
            return [("V", w.snap("V"))]

        qv = prep_q(0)
        prep_ready = {0: prep_t(0, qv)}
        for ht in range(c.HT):
            bslot = ht % 2
            if ht + 1 < c.HT:
                qv = prep_q(ht + 1)
            for sc in range(c.NSC):
                p1 = self.pfD[(2 * sc) % 4]
                p2 = self.pfD[(2 * sc + 1) % 4]
                pt1 = self._tp_tags.get(("pf", (2 * sc) % 4))
                if pt1 is not None:
                    w.wait("tensor", "V", pt1)
                pt2 = self._tp_tags.get(("pf", (2 * sc + 1) % 4))
                if pt2 is not None:
                    w.wait("tensor", "V", pt2)
                if sc == 0:
                    for tsem, tcnt in prep_ready[ht]:
                        w.wait("tensor", tsem, tcnt)
                if ht == 0:
                    w.wait("tensor", "V", xn2_sc_ready[sc])
                for cc_ in range(c.C):
                    w.ins("tensor", lambda e, cc_=cc_, sc=sc, p1=p1, bslot=bslot: e.matmul(
                        p1[:, 0:c.SCW], self.q1T[:, bslot, cc_, :],
                        self.xn2T[:, cc_, sc * c.SCW:(sc + 1) * c.SCW],
                        start=(cc_ == 0), stop=(cc_ == c.C - 1)),
                        inc="T" if cc_ == c.C - 1 else None)
                mm1 = w.snap("T")
                for cc_ in range(c.C):
                    w.ins("tensor", lambda e, cc_=cc_, sc=sc, p2=p2, bslot=bslot: e.matmul(
                        p2[:, 0:c.SCW], self.q2T[:, bslot, cc_, :],
                        self.xn2T[:, cc_, sc * c.SCW:(sc + 1) * c.SCW],
                        start=(cc_ == 0), stop=(cc_ == c.C - 1)),
                        inc="T" if cc_ == c.C - 1 else None)
                mm2 = w.snap("T")
                self._tp_tags[("qTuse", bslot)] = mm2
                w.wait("scalar", "T", mm1)
                stag = self._tp_tags.get(("sig", sc % 2))
                if stag is not None:
                    w.wait("scalar", "V", stag)
                w.ins("scalar", lambda e, sc=sc, p1=p1: e.activation(
                    out=self.sig[:, sc % 2, 0:c.SCW], in_=p1[:, 0:c.SCW], func=AF.Sigmoid), inc="S")
                s_sig = w.snap("S")
                w.wait("vector", "S", s_sig)
                w.wait("vector", "T", mm2)
                htag = self._tp_tags.get(("hidb", sc % 2))
                if htag is not None:
                    w.wait("vector", "sth0" if sc % 2 == 0 else "sth1", htag)
                w.ins("vector", lambda e, sc=sc, p2=p2: e.tensor_tensor(
                    out=self.hidb[:, sc % 2, 0:c.SCW], in0=p2[:, 0:c.SCW],
                    in1=self.sig[:, sc % 2, 0:c.SCW], op=ALU.mult), inc="V")
                v_hid = w.snap("V")
                self._tp_tags[("sig", sc % 2)] = v_hid
                self._tp_tags[("pf", (2 * sc) % 4)] = v_hid
                self._tp_tags[("pf", (2 * sc + 1) % 4)] = v_hid
                w.wait("gpsimd", "V", v_hid)
                w.ins("gpsimd", lambda e, sc=sc, ht=ht: e.dma_start(
                    out=self.hid_dram[sc * 128:(sc + 1) * 128, ht * c.SCW:(ht + 1) * c.SCW],
                    in_=self.hidb[:, sc % 2, 0:c.SCW]), inc="sth0" if sc % 2 == 0 else "sth1", by=16)
                self._tp_tags[("hidb", sc % 2)] = w.snap("sth0" if sc % 2 == 0 else "sth1")

            if ht + 1 < c.HT:
                prep_ready[ht + 1] = prep_t(ht + 1, qv)
            if w3_pending:
                k, qr = w3_pending.pop()
                self.emit_w3_tp(w, k, qr)
            due = w3_q_at.get(ht, [])
            for k in due[:-1]:
                self.emit_w3_tp(w, k, self.emit_w3_quant(w, k))
            if due:
                w3_pending.append((due[-1], self.emit_w3_quant(w, due[-1])))
        while w3_pending:
            k, qr = w3_pending.pop()
            self.emit_w3_tp(w, k, qr)

    def emit_w3_quant(self, w, et):
        c = self.cfg
        tag = self._tp_tags.get("w3f")
        if tag is not None:
            w.wait("sync", "V", tag)
        w.ins("sync", lambda e, et=et: e.dma_start(
            out=self.w3f[:, :], in_=self.w3[et * 128:(et + 1) * 128, :]), inc="ldw3", by=16)
        ld = w.snap("ldw3")
        w.wait("vector", "ldw3", ld)
        gtag = self._tp_tags.get("geb")
        if gtag is not None:
            w.wait("vector", "V", gtag)
        w.ins("vector", lambda e: e.tensor_scalar(
            out=self.geb[:, :], in0=self.w3f[:, :],
            scalar1=self.thr[2], scalar2=None, op0=ALU.is_ge), inc="V")
        q3tag = self._tp_tags.get("q3s")
        if q3tag is not None:
            w.wait("vector", "T", q3tag)
        w.ins("vector", lambda e: e.tensor_scalar(
            out=self.q3s[:, :], in0=self.w3f[:, :],
            scalar1=self.nthr[2], scalar2=None, op0=ALU.is_ge), inc="V")
        vb = w.snap("V")
        self._tp_tags["w3f"] = vb
        w.wait("vector", "V", vb)
        w.ins("vector", lambda e: e.scalar_tensor_tensor(
            out=self.q3s[:, :], in0=self.geb[:, :], scalar=-1.0,
            in1=self.q3s[:, :], op0=ALU.add, op1=ALU.add), inc="V")
        self._tp_tags["geb"] = w.snap("V")
        return w.snap("V")

    def emit_w3_tp(self, w, et, q_ready):
        c = self.cfg
        stg_tag = self._tp_tags.get("q3stage")
        for bi in range(c.HT):
            bank = self.tb[(bi // 8) % 2]
            j = bi % 8
            if j == 0:
                btag = self._tp_tags.get(("tb", (bi // 8) % 2))
                if btag is not None:
                    w.wait("tensor", "V", btag)
                if bi == 0:
                    w.wait("tensor", "V", q_ready)
                    if stg_tag is not None:
                        w.wait("tensor", "stc", w.snap("stc"))
            last = (j == 7) or (bi == c.HT - 1)
            w.ins("tensor", lambda e, bi=bi, j=j, bank=bank: e.transpose(
                bank[:, j * 128:(j + 1) * 128],
                self.q3s[:, bi * 128:(bi + 1) * 128],
                self.id_b[:, :]),
                inc="T" if last else None)
            if last:
                td = w.snap("T")
                self._tp_tags["q3s"] = td
                w.wait("vector", "T", td)
                lo = (bi // 8) * 8
                hi = bi + 1
                w.ins("vector", lambda e, lo=lo, hi=hi, bank=bank: e.tensor_copy(
                    self.q3stage[:, lo:hi, :],
                    bank[:, 0:(hi - lo) * 128].rearrange("p (j f) -> p j f", f=128)), inc="V")
                self._tp_tags[("tb", (bi // 8) % 2)] = w.snap("V")
        v_stage = w.snap("V")
        w.wait("gpsimd", "V", v_stage)
        w.ins("gpsimd", lambda e, et=et: e.dma_start(
            out=self.q3t_dram[et * 128:(et + 1) * 128, :].rearrange("p (t f) -> p t f", f=128),
            in_=self.q3stage[:, :, :]), inc="stc", by=16)
        self._tp_tags["q3stage"] = w.snap("stc")

    def ph_ffn2(self, w):
        c = self.cfg
        NTL = c.SCW // 128

        def load_hid(sc):
            slot = sc % 2
            hsem = "ldh0" if slot == 0 else "ldh1"
            htag = self._tp_tags.get(("hid", slot))
            if htag is not None:
                w.wait("sync", "T", htag)
            w.ins("sync", lambda e, sc=sc, slot=slot: e.dma_start(
                out=self.hid[:, slot, :, :],
                in_=self.hid_dram[sc * 128:(sc + 1) * 128, :]
                    .rearrange("p (t f) -> p t f", f=c.SCW)), inc=hsem, by=16)
            return hsem, w.snap(hsem)

        hid_ready = {0: load_hid(0)}
        for sc in range(c.NSC):
            hslot = sc % 2
            ld_hid = hid_ready[sc]
            otag = self._tp_tags.get("outsb")
            if otag is not None:
                w.wait("gpsimd", "stc", w.snap("stc"))
            w.ins("gpsimd", lambda e, sc=sc: e.dma_start(
                out=self.outsb[:, :, :],
                in_=self.x1_dram[:, sc * NTL * c.D:(sc + 1) * NTL * c.D]
                    .rearrange("p (t f) -> p t f", f=c.D)), inc="ldg", by=16)
            ld_x1 = w.snap("ldg")
            for et in range(c.E):
                qslot = et % 2
                qtag = self._tp_tags.get(("q3rd", qslot))
                if qtag is not None:
                    w.wait("sync", "T", qtag)
                qsem = "ldq0" if qslot == 0 else "ldq1"
                w.ins("sync", lambda e, et=et, qslot=qslot: e.dma_start(
                    out=self.q3rd[:, qslot, :, :],
                    in_=self.q3t_dram[et * 128:(et + 1) * 128, :]
                        .rearrange("p (t f) -> p t f", f=128)), inc=qsem, by=16)
                ld_q3 = w.snap(qsem)
                ab = self.accE[et % 2]
                atag = self._tp_tags.get(("acc", et % 2))
                if atag is not None:
                    w.wait("tensor", "V", atag)
                w.wait("tensor", qsem, ld_q3)
                if et == 0:
                    w.wait("tensor", ld_hid[0], ld_hid[1])
                for ht in range(c.HT):
                    w.ins("tensor", lambda e, ht=ht, qslot=qslot, ab=ab, hslot=hslot: e.matmul(
                        ab[:, 0:c.SCW], self.q3rd[:, qslot, ht, :],
                        self.hid[:, hslot, ht, :],
                        start=(ht == 0), stop=(ht == c.HT - 1)),
                        inc="T" if ht == c.HT - 1 else None)
                mm = w.snap("T")
                self._tp_tags[("q3rd", qslot)] = mm
                self._tp_tags[("hid", hslot)] = mm
                if et == 0 and sc + 1 < c.NSC:
                    hid_ready[sc + 1] = load_hid(sc + 1)
                w.wait("vector", "T", mm)
                pan_tag = self._tp_tags.get(("panel", et % 2))
                if pan_tag is not None:
                    w.wait("vector", "T", pan_tag)
                w.ins("vector", lambda e, et=et, ab=ab: e.tensor_copy(
                    self.panel[:, et % 2, 0:c.SCW], ab[:, 0:c.SCW]), inc="V")
                v_pan = w.snap("V")
                self._tp_tags[("acc", et % 2)] = v_pan
                tz = self.tpzE[et % 2]
                tztag = self._tp_tags.get(("tpz", et % 2))
                if tztag is not None:
                    w.wait("tensor", "V", tztag)
                w.wait("tensor", "V", v_pan)
                for j in range(NTL):
                    w.ins("tensor", lambda e, j=j, et=et, tz=tz: e.transpose(
                        tz[:, j * 128:(j + 1) * 128],
                        self.panel[:, et % 2, j * 128:(j + 1) * 128],
                        self.id_b[:, :]),
                        inc="T" if j == NTL - 1 else None)
                t_tz = w.snap("T")
                self._tp_tags[("panel", et % 2)] = t_tz
                w.wait("vector", "T", t_tz)
                if et == 0:
                    w.wait("vector", "ldg", ld_x1)
                for j in range(NTL):
                    w.ins("vector", lambda e, j=j, et=et, tz=tz: e.tensor_tensor(
                        out=self.outsb[:, j, et * 128:(et + 1) * 128],
                        in0=self.outsb[:, j, et * 128:(et + 1) * 128],
                        in1=tz[:, j * 128:(j + 1) * 128], op=ALU.add), inc="V")
                self._tp_tags[("tpz", et % 2)] = w.snap("V")
            v_out = w.snap("V")
            w.wait("gpsimd", "V", v_out)
            for j in range(NTL):
                w.ins("gpsimd", lambda e, j=j, sc=sc: e.dma_start(
                    out=self.out[(sc * NTL + j) * 128:(sc * NTL + j + 1) * 128, :],
                    in_=self.outsb[:, j, :]), inc="stc", by=16)
            self._tp_tags["outsb"] = w.snap("stc")
        w.wait("gpsimd", "stc", w.snap("stc"))


def build_nc(cfg):
    b = Builder(cfg)
    nc = b.nc
    b.alloc_rest()

    sems = {}
    sem_names = ["bar", "ldpro", "ldx0", "ldx1", "ldw0", "ldw1", "ldw3", "ldh", "ldh0", "ldh1",
                 "ldq0", "ldq1", "ldk0", "ldk1", "ldv0", "ldv1", "ldg", "ldg0", "ldg1", "ldg2", "ldg3", "ldg4", "ldg5", "sth0", "sth1", "sq0", "sq1", "tq0", "tq1",
                 "stc", "ccK", "ccV", "V", "T", "S", "G"]
    sem_stack = contextlib.ExitStack()
    for name in sem_names:
        sems[name] = sem_stack.enter_context(nc.semaphore(name))

    with nc.Block() as block:

        @block.sync
        def _(e):
            b.emit(Walk("sync", e, sems))

        @block.vector
        def _(e):
            b.emit(Walk("vector", e, sems))

        @block.tensor
        def _(e):
            b.emit(Walk("tensor", e, sems))

        @block.scalar
        def _(e):
            b.emit(Walk("scalar", e, sems))

        @block.gpsimd
        def _(e):
            b.emit(Walk("gpsimd", e, sems))

    b.close_all()
    sem_stack.close()
    return nc


def make_inputs(cfg, x, wq, wk, wv, wo, w1, w2, w3):
    c = cfg
    xf = np.ascontiguousarray(np.asarray(x, dtype=np.float32).reshape(c.B * c.S, c.D))
    ident = np.eye(128, dtype=np.float16)
    ones = np.ones((128, 128), dtype=np.float32)
    consts = np.tile(np.array([[1e-5, 0.5e-8]], dtype=np.float32), (128, 1))
    common = dict(
        wq=np.ascontiguousarray(np.asarray(wq, np.float32)),
        wk=np.ascontiguousarray(np.asarray(wk, np.float32)),
        wv=np.ascontiguousarray(np.asarray(wv, np.float32)),
        wo=np.ascontiguousarray(np.asarray(wo, np.float32)),
        w1=np.ascontiguousarray(np.asarray(w1, np.float32)),
        w2=np.ascontiguousarray(np.asarray(w2, np.float32)),
        w3=np.ascontiguousarray(np.asarray(w3, np.float32)),
        ident=ident, ones=ones, consts=consts,
    )
    return [dict(common, x=np.ascontiguousarray(xf[i * c.T:(i + 1) * c.T]))
            for i in range(c.ncores)]


def kernel(L0_attn_x, L0_attn_w_q, L0_attn_w_k, L0_attn_w_v, L0_attn_w_o,
           L0_ffn_w1, L0_ffn_w2, L0_ffn_w3):
    cfg = Cfg()
    nc = build_nc(cfg)
    in_maps = make_inputs(cfg, L0_attn_x, L0_attn_w_q, L0_attn_w_k, L0_attn_w_v,
                          L0_attn_w_o, L0_ffn_w1, L0_ffn_w2, L0_ffn_w3)
    res = run_bass_kernel_spmd(nc, in_maps, core_ids=list(range(cfg.ncores)))
    shards = [np.asarray(res.results[i]["out"], dtype=np.float32).reshape(cfg.T, cfg.D)
              for i in range(cfg.ncores)]
    return np.concatenate(shards, axis=0).reshape(cfg.B, cfg.S, cfg.D)


# version 22
# speedup vs baseline: 1.0144x; 1.0100x over previous
import contextlib
from collections import defaultdict

import numpy as np
import ml_dtypes
import concourse.bass as bass
import concourse.mybir as mybir
from concourse.bass_utils import run_bass_kernel_spmd

F32 = mybir.dt.float32
BF16 = mybir.dt.float16
AF = mybir.ActivationFunctionType
ALU = mybir.AluOpType
AX = mybir.AxisListType

ENGINES = ("sync", "vector", "tensor", "scalar", "gpsimd")


class Cfg:
    def __init__(self, B=2, S=8192, D=2048, H=16, HF=5632, ncores=8, dbg=False):
        self.B, self.S, self.D, self.H, self.HF = B, S, D, H, HF
        self.ncores = ncores
        self.R = ncores // B
        self.T = B * S // ncores
        self.NT = self.T // 128
        self.C = D // 128
        self.E = D // 128
        self.HT = HF // 128
        self.KT = S // 128
        self.DH = D // H
        assert self.DH == 128
        self.QCW = min(512, self.T)
        self.NQC = self.T // self.QCW
        self.G = 2
        assert self.KT % self.G == 0
        self.NG = self.KT // self.G
        self.SCW = min(512, self.T)
        self.NSC = self.T // self.SCW
        self.SCALE = 1.0 / float(np.sqrt(np.float64(self.DH)))
        self.dbg = dbg


class Walk:

    def __init__(self, name, eng, sems):
        self.n = name
        self.e = eng
        self.h = sems
        self.c = defaultdict(int)

    def ins(self, eng, f, inc=None, by=1):
        if self.n == eng:
            r = f(self.e)
            if inc is not None:
                r.then_inc(self.h[inc], by)
        if inc is not None:
            self.c[inc] += by

    def wait(self, eng, sem, val):
        if val is None or val <= 0:
            return
        if self.n == eng:
            self.e.wait_ge(self.h[sem], val)

    def snap(self, sem):
        return self.c[sem]

    def barrier(self):
        for eng in ENGINES:
            self.ins(eng, lambda e: e.nop(), inc="bar")
        v = self.c["bar"]
        for eng in ENGINES:
            self.wait(eng, "bar", v)
            for s_ in ("stc", "sth0", "sth1", "se0", "se1"):
                self.wait(eng, s_, self.c[s_])


class Builder:
    def __init__(self, cfg):
        self.cfg = cfg
        nc = bass.Bass("TRN2", target_bir_lowering=False)
        self.nc = nc
        c = cfg

        dt = nc.dram_tensor
        self.x = dt("x", [c.T, c.D], F32, kind="ExternalInput")
        self.wq = dt("wq", [c.D, c.D], F32, kind="ExternalInput")
        self.wk = dt("wk", [c.D, c.D], F32, kind="ExternalInput")
        self.wv = dt("wv", [c.D, c.D], F32, kind="ExternalInput")
        self.wo = dt("wo", [c.D, c.D], F32, kind="ExternalInput")
        self.w1 = dt("w1", [c.HF, c.D], F32, kind="ExternalInput")
        self.w2 = dt("w2", [c.HF, c.D], F32, kind="ExternalInput")
        self.w3 = dt("w3", [c.D, c.HF], F32, kind="ExternalInput")
        self.ident = dt("ident", [128, 128], BF16, kind="ExternalInput")
        self.consts = dt("consts", [128, 2], F32, kind="ExternalInput")
        self.ones = dt("ones", [128, 128], F32, kind="ExternalInput")
        self.out = dt("out", [c.T, c.D], F32, kind="ExternalOutput")

        self.cc_in_k = dt("cc_in_k", [c.D, c.T], BF16)
        self.cc_out_k = dt("cc_out_k", [c.R * c.D, c.T], BF16)
        self.cc_in_v = dt("cc_in_v", [c.T, c.D], BF16)
        self.cc_out_v = dt("cc_out_v", [c.R * c.T, c.D], BF16)
        self.x1_dram = dt("x1_dram", [128, c.NT * c.D], BF16)
        self.xn2_dram = dt("xn2_dram", [128, c.NT * c.D], BF16)
        self.hid_dram = dt("hid_dram", [c.NSC * 128, c.HT * c.SCW], BF16)
        self.q3t_dram = dt("q3t_dram", [c.E * 128, c.HT * 128], BF16)
        self.q12_dram = [dt(f"q12_dram{i}", [128, c.D], BF16) for i in range(2)]
        if c.dbg:
            self.dbg_xnT = dt("dbg_xnT", [128, c.C * c.T], BF16, kind="ExternalOutput")
            self.dbg_qT = dt("dbg_qT", [128, c.C * c.T], BF16, kind="ExternalOutput")
            self.dbg_att = dt("dbg_att", [128, c.C * c.T], BF16, kind="ExternalOutput")
            self.dbg_thr = dt("dbg_thr", [128, 3], F32, kind="ExternalOutput")

        st = nc.sbuf_tensor
        self.root = contextlib.ExitStack()
        r = self.root.enter_context
        self.id_b = r(st("id_b", [128, 128], BF16))
        self.ones_b = r(st("ones_b", [128, 128], BF16))
        self.cst = r(st("cst", [128, 2], F32))
        self.ones_f = r(st("ones_f", [128, 128], F32))
        self.xstrip = r(st("xstrip", [128, 2, c.D], BF16))
        self.sgnb = r(st("sgnb", [128, c.D], BF16))
        self.nb = r(st("nb", [128, 2, c.D], BF16))
        self.sqb = r(st("sqb", [128, 1024], BF16))
        self.smalls = r(st("smalls", [128, 34], F32))
        sm = self.smalls
        self.sx = sm[:, 0:1]
        self.ssq = sm[:, 1:2]
        self.mn = sm[:, 2:3]
        self.m2 = sm[:, 3:4]
        self.var = sm[:, 4:5]
        self.std = sm[:, 5:6]
        self.rstd = sm[:, 6:7]
        self.gtmp = sm[:, 7:8]
        self.gacc = [sm[:, 8 + i:9 + i] for i in range(3)]
        self.thr = [sm[:, 11 + i:12 + i] for i in range(3)]
        self.nthr = [sm[:, 14 + i:15 + i] for i in range(3)]
        self.gtmpV = sm[:, 17:18]
        self.gaccV = [sm[:, 18 + i:19 + i] for i in range(3)]
        self.lns = []
        for b_ in range(2):
            o_ = 21 + 6 * b_
            self.lns.append(dict(sx=sm[:, o_:o_ + 1], ssq=sm[:, o_ + 1:o_ + 2],
                                 mn=sm[:, o_ + 2:o_ + 3], m2=sm[:, o_ + 3:o_ + 4],
                                 var=sm[:, o_ + 4:o_ + 5], rstd=sm[:, o_ + 5:o_ + 6]))

        self.scope_qt = contextlib.ExitStack()
        self.QT = self.scope_qt.enter_context(st("QT", [128, c.C, c.T], BF16))

        self.scope_a = contextlib.ExitStack()
        self.xnT = self.scope_a.enter_context(st("xnT", [128, c.C, c.T], BF16))
        self.wTv = self.scope_a.enter_context(st("wTv", [128, c.C, 512], BF16))
        self.eb = self.scope_a.enter_context(st("eb", [128, 2, max(c.T, c.D // 2)], BF16))
        self.wstripA = self.scope_a.enter_context(st("wstripA", [128, 2, c.D], F32))
        self.wbfA = self.scope_a.enter_context(st("wbfA", [128, 2, c.D], BF16))
        self.wTA = self.scope_a.enter_context(st("wTA", [128, 2, c.C, 128], BF16))

        self.ps_a = contextlib.ExitStack()
        pt = nc.psum_tensor
        self.tbA = [self.ps_a.enter_context(pt(f"tbA{i}", [128, 1024], BF16)) for i in range(2)]
        self.pkA = [self.ps_a.enter_context(pt(f"pkA{i}", [128, 512], F32)) for i in range(2)]

    def alloc_rest(self):
        nc, c = self.nc, self.cfg
        st, pt = nc.sbuf_tensor, nc.psum_tensor
        self.scope_a.close()
        self.ps_a.close()
        self.scope_b = contextlib.ExitStack()
        r = self.scope_b.enter_context
        self.KH = r(st("KH", [128, 2, c.R * c.T], BF16))
        self.VH = r(st("VH", [128, 2, c.KT, 128], BF16))
        self.gbuf = r(st("gbuf", [128, 6, 1024], F32))
        self.PT = r(st("PT", [128, 6, c.G * c.QCW], BF16))
        self.Zacc = [r(st(f"Zacc{p_}", [128, 2 * c.QCW], BF16)) for p_ in range(2)]
        self.rz = r(st("rz", [128, c.QCW], F32))
        self.rzh = r(st("rzh", [128, c.QCW], BF16))
        self.ps_b = contextlib.ExitStack()
        self.Sp = [self.ps_b.enter_context(pt(f"Sp{i}", [128, c.G * c.QCW], F32)) for i in range(2)]
        self.Op = [self.ps_b.enter_context(pt(f"Op{i}", [128, c.QCW], F32)) for i in range(2)]
        self.ZB = [self.ps_b.enter_context(pt(f"ZB{i}", [128, 512], F32)) for i in range(2)]
        self.scope_b.close()
        self.ps_b.close()
        self.scope_c = contextlib.ExitStack()
        self.wTo = self.scope_c.enter_context(st("wTo", [128, c.C, c.D], BF16))
        self.x1f = self.scope_c.enter_context(st("x1f", [128, c.D], F32))
        self.x1c = self.sgnb
        self.wstripC = self.scope_c.enter_context(st("wstripC", [128, 2, c.D], F32))
        self.wbfC = self.scope_c.enter_context(st("wbfC", [128, 2, c.D], BF16))
        self.wTC = self.scope_c.enter_context(st("wTC", [128, 2, c.C, 128], BF16))
        self.ps_c = contextlib.ExitStack()
        self.tbC = [self.ps_c.enter_context(pt(f"tbC{i}", [128, 1024], BF16)) for i in range(2)]
        self.poC = self.ps_c.enter_context(pt("poC", [128, c.D], F32))
        self.scope_c.close()
        self.scope_qt.close()
        self.ps_c.close()
        self.scope_e = contextlib.ExitStack()
        r = self.scope_e.enter_context
        self.xn2T = r(st("xn2T", [128, c.C, c.T], BF16))
        self.w3f = r(st("w3f", [128, c.HF], F32))
        self.q3s = r(st("q3s", [128, c.HF], BF16))
        self.q3stage = r(st("q3stage", [128, c.HT, 128], BF16))
        self.wstrip2 = r(st("wstrip2", [128, 2, c.D], F32))
        self.qbf = r(st("qbf", [128, 2, c.D], BF16))
        self.q1T = r(st("q1T", [128, 2, c.C, 128], BF16))
        self.q2T = r(st("q2T", [128, 2, c.C, 128], BF16))
        self.geb = r(st("geb", [128, c.HF], BF16))
        self.sig = r(st("sig", [128, 2, 512], F32))
        self.hidb = r(st("hidb", [128, 2, 512], BF16))
        self.ps_d = contextlib.ExitStack()
        p = self.ps_d.enter_context
        self.tbD = [p(pt(f"tbD{i}", [128, 1024], BF16)) for i in range(2)]
        self.pfD = [p(pt(f"pfD{i}", [128, 512], F32)) for i in range(4)]
        self.scope_e.close()
        self.ps_d.close()
        self.scope_f = contextlib.ExitStack()
        r = self.scope_f.enter_context
        self.hid = r(st("hid", [128, 2, c.HT, c.SCW], BF16))
        self.q3rd = r(st("q3rd", [128, 2, c.HT, 128], BF16))
        self.outsb = r(st("outsb", [128, c.SCW // 128, c.D], F32))
        self.panel = r(st("panel", [128, 2, 512], BF16))
        self.ps_e = contextlib.ExitStack()
        p = self.ps_e.enter_context
        self.accE = [p(pt(f"accE{i}", [128, 512], F32)) for i in range(2)]
        self.tpzE = [p(pt(f"tpzE{i}", [128, 512], BF16)) for i in range(2)]

    def close_all(self):
        self.scope_f.close()
        self.ps_e.close()
        self.root.close()

    def _lp(self, f):
        def g(e):
            with self.nc.allow_low_precision(reason="fp16 softmax partial sums (error ~3e-4 rel, inside tolerance)"):
                return f(e)
        return g

    def _replica_groups(self):
        c = self.cfg
        return [list(range(b * c.R, (b + 1) * c.R)) for b in range(c.B)]

    def emit(self, w):
        self._tp_tags = {}
        self._pt_pv = None
        self._pt_z = None
        self.tb, self.pk = self.tbA, self.pkA
        self.wstrip, self.wbf, self.wT = self.wstripA, self.wbfA, self.wTA
        self.ph_ln1(w)
        self.ph_proj(w)
        w.barrier()
        self.ph_attn(w)
        w.barrier()
        self.tb = self.tbC
        self.wstrip, self.wbf, self.wT = self.wstripC, self.wbfC, self.wTC
        self.ph_outproj(w)
        w.barrier()
        self.tb = self.tbD
        self.ph_ffn1(w)
        w.barrier()
        self.ph_ffn2(w)

    def ln_tile(self, w, xin_f, out_bf, out_war=None, idx=0, disc=None, eps=1e-5):
        c = self.cfg
        D = c.D
        s = self.lns[idx]
        sqb = disc
        w.ins("scalar", lambda e, sqb=sqb: e.activation(out=sqb, in_=xin_f[:, 0:c.D // 2],
                                               func=AF.Square, accum_out=s["ssq"]), inc="S")
        w.wait("scalar", "S", w.snap("S"))
        w.ins("scalar", lambda e, sqb=sqb: e.activation(out=sqb, in_=xin_f[:, c.D // 2:c.D],
                                               func=AF.Square, accum_out=s["m2"]), inc="S")
        sq_done = w.snap("S")
        w.ins("vector", lambda e: e.tensor_reduce(out=s["sx"], in_=xin_f,
                                                  axis=AX.X, op=ALU.add), inc="V")
        w.wait("vector", "V", w.snap("V"))
        w.ins("vector", lambda e: e.tensor_scalar(out=s["mn"], in0=s["sx"],
                                                  scalar1=1.0 / D, scalar2=None, op0=ALU.mult), inc="V")
        w.wait("vector", "S", sq_done)
        w.wait("vector", "V", w.snap("V"))
        w.ins("vector", lambda e: e.tensor_tensor(out=s["ssq"], in0=s["ssq"],
                                                  in1=s["m2"], op=ALU.add), inc="V")
        w.wait("vector", "V", w.snap("V"))
        w.ins("vector", lambda e: e.tensor_tensor(out=s["m2"], in0=s["mn"],
                                                  in1=s["mn"], op=ALU.mult), inc="V")
        w.wait("vector", "V", w.snap("V"))
        w.ins("vector", lambda e: e.scalar_tensor_tensor(out=s["var"], in0=s["ssq"],
                                                         scalar=1.0 / D, in1=s["m2"],
                                                         op0=ALU.mult, op1=ALU.subtract), inc="V")
        v_var = w.snap("V")
        w.wait("scalar", "V", v_var)
        w.ins("scalar", lambda e: e.activation(out=s["var"], in_=s["var"],
                                               func=AF.Sqrt, bias=self.cst[:, 0:1]), inc="S")
        s_sqrt = w.snap("S")
        w.wait("vector", "S", s_sqrt)
        w.ins("vector", lambda e: e.reciprocal(out=s["rstd"], in_=s["var"]), inc="V")
        w.wait("vector", "V", w.snap("V"))
        if out_war is not None:
            w.wait("vector", out_war[0], out_war[1])
        w.ins("vector", lambda e: e.tensor_scalar(out=out_bf, in0=xin_f,
                                                  scalar1=s["mn"], scalar2=s["rstd"],
                                                  op0=ALU.subtract, op1=ALU.mult), inc="V")
        return w.snap("V")

    def transpose_tile_to(self, w, src_bf, dst, st_idx, wait_v=None):
        c = self.cfg
        for half in range(c.C // 8):
            bank = self.tb[half % 2]
            tag = self._tp_tags.get(("tb", half % 2))
            if tag is not None:
                w.wait("tensor", "V", tag)
            if wait_v is not None:
                w.wait("tensor", "V", wait_v)
                wait_v = None
            for j in range(8):
                cc_ = half * 8 + j
                w.ins("tensor", lambda e, cc_=cc_, j=j, bank=bank: e.transpose(
                    bank[:, j * 128:(j + 1) * 128],
                    src_bf[:, cc_ * 128:(cc_ + 1) * 128],
                    self.id_b[:, :]),
                    inc="T" if j == 7 else None)
            t_done = w.snap("T")
            w.wait("vector", "T", t_done)
            w.ins("vector", lambda e, half=half, bank=bank: e.tensor_copy(
                dst[:, half * 8:(half + 1) * 8, st_idx * 128:(st_idx + 1) * 128],
                bank[:, :].rearrange("p (j f) -> p j f", f=128)), inc="V")
            self._tp_tags[("tb", half % 2)] = w.snap("V")
        return w.snap("T")

    def wstrip_prep(self, w, src_dram, row0, slot, dst=None):
        c = self.cfg
        tag = self._tp_tags.get(("wstrip", slot))
        if tag is not None:
            w.wait("sync", "V", tag)
        lsem = "ldw0" if slot == 0 else "ldw1"
        w.ins("sync", lambda e: e.dma_start(out=self.wstrip[:, slot, :],
                                            in_=src_dram[row0:row0 + 128, :]), inc=lsem, by=16)
        ld = w.snap(lsem)
        w.wait("vector", lsem, ld)
        w.ins("vector", lambda e: e.tensor_copy(self.wbf[:, slot, :], self.wstrip[:, slot, :]), inc="V")
        self._tp_tags[("wstrip", slot)] = w.snap("V")
        cv = w.snap("V")
        for half in range(c.C // 8):
            bank = self.tb[half % 2]
            btag = self._tp_tags.get(("tb", half % 2))
            if btag is not None:
                w.wait("tensor", "V", btag)
            if half == 0:
                w.wait("tensor", "V", cv)
            for j in range(8):
                cc_ = half * 8 + j
                w.ins("tensor", lambda e, cc_=cc_, j=j, bank=bank: e.transpose(
                    bank[:, j * 128:(j + 1) * 128],
                    self.wbf[:, slot, cc_ * 128:(cc_ + 1) * 128],
                    self.id_b[:, :]),
                    inc="T" if j == 7 else None)
            td = w.snap("T")
            w.wait("vector", "T", td)
            if dst is None:
                w.ins("vector", lambda e, half=half, bank=bank: e.tensor_copy(
                    self.wT[:, slot, half * 8:(half + 1) * 8, :],
                    bank[:, :].rearrange("p (j f) -> p j f", f=128)), inc="V")
            else:
                w.ins("vector", lambda e, half=half, bank=bank: e.tensor_copy(
                    dst(half), bank[:, :].rearrange("p (j f) -> p j f", f=128)), inc="V")
            self._tp_tags[("tb", half % 2)] = w.snap("V")
        return w.snap("V")

    def ph_ln1(self, w):
        c = self.cfg
        w.ins("sync", lambda e: e.dma_start(out=self.id_b[:, :], in_=self.ident[:, :]), inc="ldpro", by=16)
        w.ins("sync", lambda e: e.dma_start(out=self.ones_f[:, :], in_=self.ones[:, :]), inc="ldpro", by=16)
        w.ins("sync", lambda e: e.dma_start(out=self.cst[:, :], in_=self.consts[:, :]), inc="ldpro", by=16)
        ld0 = w.snap("ldpro")
        for eng in ("vector", "tensor", "scalar", "gpsimd"):
            w.wait(eng, "ldpro", ld0)
        w.ins("vector", lambda e: e.tensor_copy(self.ones_b[:, :], self.ones_f[:, :]), inc="V")
        self.v_onesb = w.snap("V")
        w.wait("tensor", "V", self.v_onesb)
        for stt in range(c.NT):
            slot = stt % 2
            tag = self._tp_tags.get(("xstrip", slot))
            if tag is not None:
                w.wait("gpsimd", "V", tag)
            xsem = "ldx0" if slot == 0 else "ldx1"
            w.ins("gpsimd", lambda e, stt=stt, slot=slot: e.dma_start(
                out=self.xstrip[:, slot, :],
                in_=self.x[stt * 128:(stt + 1) * 128, :]), inc=xsem, by=16)
            ld = w.snap(xsem)
            w.wait("vector", xsem, ld)
            w.wait("scalar", xsem, ld)
            nb_war = self._tp_tags.get(("nb", slot))
            vdone = self.ln_tile(w, self.xstrip[:, slot, :], self.nb[:, slot, :],
                                 out_war=("T", nb_war) if nb_war else None, idx=slot,
                                 disc=self.eb[:, slot, 0:c.D // 2])
            self._tp_tags[("xstrip", slot)] = vdone
            t_tp = self.transpose_tile_to(w, self.nb[:, slot, :], self.xnT, stt, wait_v=vdone)
            self._tp_tags[("nb", slot)] = t_tp
        self.v_xnT_done = w.snap("V")
        if c.dbg:
            w.wait("gpsimd", "V", self.v_xnT_done)
            w.ins("gpsimd", lambda e: e.dma_start(
                out=self.dbg_xnT[:, :].rearrange("p (a b) -> p a b", b=c.T),
                in_=self.xnT[:, :, :]), inc="stc", by=16)
            w.wait("gpsimd", "stc", w.snap("stc"))

    def ph_proj(self, w):
        c = self.cfg
        scw = min(512, c.T)
        nsc = c.T // scw

        for et in range(c.E):
            slot = et % 2
            rdy = self.wstrip_prep(w, self.wk, et * 128, slot)
            for sc in range(nsc):
                pbank = self.pk[sc % 2]
                ptag = self._tp_tags.get(("pk", sc % 2))
                if ptag is not None:
                    w.wait("tensor", "V", ptag)
                if sc == 0:
                    w.wait("tensor", "V", max(rdy, self.v_xnT_done))
                for cc_ in range(c.C):
                    w.ins("tensor", lambda e, cc_=cc_, sc=sc, slot=slot, pbank=pbank: e.matmul(
                        pbank[:, 0:scw],
                        self.wT[:, slot, cc_, :],
                        self.xnT[:, cc_, sc * scw:(sc + 1) * scw],
                        start=(cc_ == 0), stop=(cc_ == c.C - 1)),
                        inc="T" if cc_ == c.C - 1 else None)
                mm = w.snap("T")
                w.wait("vector", "T", mm)
                etag = self._tp_tags.get(("eb", slot))
                if etag is not None:
                    w.wait("vector", "se0" if slot == 0 else "se1", etag)
                w.ins("vector", lambda e, sc=sc, slot=slot, pbank=pbank: e.tensor_copy(
                    self.eb[:, slot, sc * scw:(sc + 1) * scw], pbank[:, 0:scw]), inc="V")
                self._tp_tags[("pk", sc % 2)] = w.snap("V")
            row_done = w.snap("V")
            w.wait("gpsimd", "V", row_done)
            w.ins("gpsimd", lambda e, et=et, slot=slot: e.dma_start(
                out=self.cc_in_k[et * 128:(et + 1) * 128, :],
                in_=self.eb[:, slot, 0:c.T]), inc="se0" if slot == 0 else "se1", by=16)
            self._tp_tags[("eb", slot)] = w.snap("se0" if slot == 0 else "se1")
        w.wait("gpsimd", "se0", w.snap("se0"))
        w.wait("gpsimd", "se1", w.snap("se1"))
        w.ins("gpsimd", lambda e: e.collective_compute(
            "AllGather", ALU.bypass,
            replica_groups=self._replica_groups(),
            ins=[self.cc_in_k.ap().opt()], outs=[self.cc_out_k.ap().opt()]), inc="ccK")

        for ec in range(c.D // 512):
            for j in range(4):
                et = ec * 4 + j
                slot = et % 2
                wtv_tag = self._tp_tags.get("wTv")
                if wtv_tag is not None and j == 0:
                    w.wait("vector", "T", wtv_tag)
                self.wstrip_prep(w, self.wv, et * 128, slot,
                                 dst=lambda half, j=j: self.wTv[:, half * 8:(half + 1) * 8,
                                                               j * 128:(j + 1) * 128])
            wtv_ready = w.snap("V")
            for stt in range(c.NT):
                pbank = self.pk[stt % 2]
                ptag = self._tp_tags.get(("pk", stt % 2))
                if ptag is not None:
                    w.wait("tensor", "V", ptag)
                if stt == 0:
                    w.wait("tensor", "V", wtv_ready)
                for cc_ in range(c.C):
                    w.ins("tensor", lambda e, cc_=cc_, stt=stt, pbank=pbank: e.matmul(
                        pbank[:, :],
                        self.xnT[:, cc_, stt * 128:(stt + 1) * 128],
                        self.wTv[:, cc_, :],
                        start=(cc_ == 0), stop=(cc_ == c.C - 1)),
                        inc="T" if cc_ == c.C - 1 else None)
                mm = w.snap("T")
                self._tp_tags["wTv"] = mm
                w.wait("vector", "T", mm)
                eslot = stt % 2
                etag = self._tp_tags.get(("eb", eslot))
                if etag is not None:
                    w.wait("vector", "se0" if eslot == 0 else "se1", etag)
                w.ins("vector", lambda e, eslot=eslot, pbank=pbank: e.tensor_copy(
                    self.eb[:, eslot, 0:512], pbank[:, :]), inc="V")
                vdone = w.snap("V")
                self._tp_tags[("pk", stt % 2)] = vdone
                w.wait("gpsimd", "V", vdone)
                w.ins("gpsimd", lambda e, stt=stt, ec=ec, eslot=eslot: e.dma_start(
                    out=self.cc_in_v[stt * 128:(stt + 1) * 128, ec * 512:(ec + 1) * 512],
                    in_=self.eb[:, eslot, 0:512]), inc="se0" if eslot == 0 else "se1", by=16)
                self._tp_tags[("eb", eslot)] = w.snap("se0" if eslot == 0 else "se1")
        w.wait("gpsimd", "se0", w.snap("se0"))
        w.wait("gpsimd", "se1", w.snap("se1"))
        w.ins("gpsimd", lambda e: e.collective_compute(
            "AllGather", ALU.bypass,
            replica_groups=self._replica_groups(),
            ins=[self.cc_in_v.ap().opt()], outs=[self.cc_out_v.ap().opt()]), inc="ccV")

        for et in range(c.E):
            slot = et % 2
            rdy = self.wstrip_prep(w, self.wq, et * 128, slot)
            for sc in range(nsc):
                pbank = self.pk[sc % 2]
                ptag = self._tp_tags.get(("pk", sc % 2))
                if ptag is not None:
                    w.wait("tensor", "V", ptag)
                if sc == 0:
                    w.wait("tensor", "V", rdy)
                for cc_ in range(c.C):
                    w.ins("tensor", lambda e, cc_=cc_, sc=sc, slot=slot, pbank=pbank: e.matmul(
                        pbank[:, 0:scw],
                        self.wT[:, slot, cc_, :],
                        self.xnT[:, cc_, sc * scw:(sc + 1) * scw],
                        start=(cc_ == 0), stop=(cc_ == c.C - 1)),
                        inc="T" if cc_ == c.C - 1 else None)
                mm = w.snap("T")
                w.wait("vector", "T", mm)
                w.ins("vector", lambda e, et=et, sc=sc, pbank=pbank: e.tensor_scalar(
                    out=self.QT[:, et, sc * scw:(sc + 1) * scw], in0=pbank[:, 0:scw],
                    scalar1=float(c.SCALE), scalar2=None, op0=ALU.mult), inc="V")
                self._tp_tags[("pk", sc % 2)] = w.snap("V")
        if c.dbg:
            v_q = w.snap("V")
            w.wait("gpsimd", "V", v_q)
            w.ins("gpsimd", lambda e: e.dma_start(
                out=self.dbg_qT[:, :].rearrange("p (a b) -> p a b", b=c.T),
                in_=self.QT[:, :, :]), inc="stc", by=16)
            w.wait("gpsimd", "stc", w.snap("stc"))

    def ph_attn(self, w):
        c = self.cfg
        jobs = self._gamma_jobs()
        n_iter = c.H * c.NQC
        per = min(-(-len(jobs) // n_iter), 6)
        job_idx = [0]
        gam_first = [True, True, True]
        gam_firstV = [True, True, True]
        gslot_tag = {}

        last_exp = 0
        evac_snap = {}
        zb_free = 0
        prev_zmm = [0, 0]
        pt_user = {}
        pending_fin = [None]

        def fin_stage1(w):
            st_ = pending_fin[0]
            if st_ is None or st_.get("stage") != 0:
                return
            (fh, fqc, fgi, fob, fz_last, fpar) = st_["t"]
            zb = self.ZB[fpar]
            w.wait("tensor", "V", fz_last["vector"])
            w.wait("tensor", "V", zb_free[fpar])
            for i in range(2):
                w.ins("tensor", lambda e, i=i, fpar=fpar, zb=zb: e.matmul(
                    zb[0:1, 0:c.QCW], self.ones_b[:, 0:1],
                    self.Zacc[fpar][:, i * c.QCW:(i + 1) * c.QCW],
                    start=(i == 0), stop=(i == 1), skip_group_check=True),
                    inc="T" if i == 1 else None)
            z_mm = w.snap("T")
            prev_zmm[fpar] = z_mm
            w.wait("vector", "T", z_mm)
            w.wait("vector", "T", last_bcast[0])
            w.ins("vector", self._lp(lambda e, zb=zb: e.reciprocal(out=self.rzh[0:1, :], in_=zb[0:1, 0:c.QCW])), inc="V")
            st_["v_recip"] = w.snap("V")
            st_["stage"] = 1

        def fin_stage2(w):
            st_ = pending_fin[0]
            if st_ is None or st_.get("stage") != 1:
                return
            pending_fin[0] = None
            (fh, fqc, fgi, fob, fz_last, fpar) = st_["t"]
            zb = self.ZB[fpar]
            w.wait("tensor", "V", st_["v_recip"])
            w.ins("tensor", lambda e, zb=zb: e.matmul(
                zb[:, 0:c.QCW], self.ones_b[0:1, :], self.rzh[0:1, :],
                start=True, stop=True, skip_group_check=True), inc="T")
            bc_mm = w.snap("T")
            last_bcast[0] = bc_mm
            w.wait("vector", "T", bc_mm)
            w.wait("vector", "V", last_evac[0])
            w.ins("vector", lambda e, zb=zb: e.tensor_copy(self.rz[:, :], zb[:, 0:c.QCW]), inc="V")
            zb_free[fpar] = w.snap("V")
            w.wait("vector", "T", fz_last["pv_done"])
            w.wait("vector", "V", zb_free[fpar])
            w.ins("vector", lambda e, fh=fh, fqc=fqc, fob=fob: e.tensor_tensor(
                out=self.QT[:, fh, fqc * c.QCW:(fqc + 1) * c.QCW],
                in0=fob[:, :], in1=self.rz[:, :], op=ALU.mult), inc="V")
            evac_snap[("o", fgi % 2, fgi)] = w.snap("V")
            last_evac[0] = w.snap("V")

        zb_free = [0, 0]
        last_evac = [0]
        last_bcast = [0]
        exp_of_slot = [0, 0]
        for h in range(c.H):
            hb = h % 2
            ktag = self._tp_tags.get(("kh", hb))
            if ktag is not None:
                w.wait("sync", "T", ktag)
            if h == 0:
                w.wait("sync", "ccK", 1)
            ksem = "ldk0" if hb == 0 else "ldk1"
            vsem = "ldv0" if hb == 0 else "ldv1"
            for r_ in range(c.R):
                w.ins("sync", lambda e, h=h, hb=hb, r_=r_: e.dma_start(
                    out=self.KH[:, hb, r_ * c.T:(r_ + 1) * c.T],
                    in_=self.cc_out_k[r_ * c.D + h * 128: r_ * c.D + (h + 1) * 128, :]),
                    inc=ksem, by=16)
            if h == 0:
                w.wait("sync", "ccV", 1)
            ktpr = c.KT // c.R
            for r_ in range(c.R):
                w.ins("sync", lambda e, h=h, hb=hb, r_=r_, ktpr=ktpr: e.dma_start(
                    out=self.VH[:, hb, r_ * ktpr:(r_ + 1) * ktpr, :],
                    in_=self.cc_out_v[r_ * c.T:(r_ + 1) * c.T, h * 128:(h + 1) * 128]
                        .rearrange("(t p) e -> p t e", p=128)),
                    inc=vsem, by=16)
            ldk_h = w.snap(ksem)
            ldv_h = w.snap(vsem)

            for qc in range(c.NQC):
                gi_ = h * c.NQC + qc
                par = gi_ % 2
                ob = self.Op[gi_ % 2]
                my_jobs = []
                for k_ in range(per):
                    if job_idx[0] < len(jobs):
                        my_jobs.append((job_idx[0], jobs[job_idx[0]]))
                        job_idx[0] += 1
                for ji, (gidx, apf, width) in my_jobs:
                    slot = ji % 6
                    gsem = f"ldg{slot}"
                    tag = gslot_tag.get(slot)
                    if tag is not None:
                        w.wait("gpsimd", tag[0], tag[1])
                    w.ins("gpsimd", lambda e, apf=apf, slot=slot, width=width: e.dma_start(
                        out=self.gbuf[:, slot, 0:width], in_=apf()), inc=gsem, by=16)

                def gamma_one(ji, gidx, width):
                    slot = ji % 6
                    gsem = f"ldg{slot}"
                    if ji % 2 == 0:
                        geng, gctr = "vector", "V"
                    else:
                        geng, gctr = "scalar", "S"
                    w.wait(geng, gsem, (ji // 6 + 1) * 16)
                    w.wait(geng, gctr, w.snap(gctr))
                    if geng == "scalar":
                        w.ins("scalar", lambda e, slot=slot, width=width: e.activation(
                            out=self.sqb[:, 0:width], in_=self.gbuf[:, slot, 0:width],
                            func=AF.Abs, accum_out=self.gtmp), inc="S")
                    else:
                        w.ins("vector", lambda e, slot=slot, width=width: e.tensor_reduce(
                            out=self.gtmpV, in_=self.gbuf[:, slot, 0:width],
                            axis=AX.X, op=ALU.add, apply_absolute_value=True), inc="V")
                    gslot_tag[slot] = (gctr, w.snap(gctr))
                    w.wait(geng, gctr, w.snap(gctr))
                    gt = self.gtmp if geng == "scalar" else self.gtmpV
                    ga = self.gacc if geng == "scalar" else self.gaccV
                    gfirst = gam_first if geng == "scalar" else gam_firstV
                    if gfirst[gidx]:
                        if geng == "scalar":
                            w.ins("scalar", lambda e, gidx=gidx, gt=gt, ga=ga: e.copy(ga[gidx], gt), inc="S")
                        else:
                            w.ins("vector", lambda e, gidx=gidx, gt=gt, ga=ga: e.tensor_copy(ga[gidx], gt), inc="V")
                        gfirst[gidx] = False
                    else:
                        if geng == "scalar":
                            w.ins("scalar", lambda e, gidx=gidx, gt=gt, ga=ga: e.add(ga[gidx], ga[gidx], gt), inc="S")
                        else:
                            w.ins("vector", lambda e, gidx=gidx, gt=gt, ga=ga: e.tensor_tensor(
                                out=ga[gidx], in0=ga[gidx], in1=gt, op=ALU.add), inc="V")

                first_z = {"vector": True}
                z_last = {"vector": None, "pv_done": None}
                exp_snap = {}
                pt_z_of = {}
                for g in range(c.NG + 1):
                    if g < c.NG:
                        pb = g % 6
                        sslot = g % 2
                        sp = self.Sp[sslot]
                        w.wait("tensor", "S", exp_of_slot[sslot])
                        if g == 0 and qc == 0:
                            w.wait("tensor", ksem, ldk_h)
                        if g == 0:
                            prev = evac_snap.get(("o", gi_ % 2, gi_ - 2))
                            if prev is not None:
                                w.wait("tensor", "V", prev)
                        for i in range(c.G):
                            kt = g * c.G + i
                            w.ins("tensor", lambda e, kt=kt, i=i, qc=qc, hb=hb, h=h, sp=sp: e.matmul(
                                sp[:, i * c.QCW:(i + 1) * c.QCW],
                                self.KH[:, hb, kt * 128:(kt + 1) * 128],
                                self.QT[:, h, qc * c.QCW:(qc + 1) * c.QCW],
                                start=True, stop=True, skip_group_check=True),
                                inc="T" if i == c.G - 1 else None)
                        s_mm = w.snap("T")
                        pu = pt_user.get(pb)
                        tw = s_mm if pu is None else max(s_mm, pu[0])
                        w.wait("scalar", "T", tw)
                        if pu is not None:
                            w.wait("scalar", pu[1], pu[2])
                        w.ins("scalar", lambda e, pb=pb, sp=sp: e.activation(
                            out=self.PT[:, pb, :], in_=sp[:, :], func=AF.Exp), inc="S")
                        last_exp = w.snap("S")
                        exp_of_slot[sslot] = last_exp
                        zeng = "vector"
                        zctr = "V"
                        acc = self.Zacc[par]
                        w.wait(zeng, "S", last_exp)
                        if z_last[zeng] is not None:
                            w.wait(zeng, zctr, z_last[zeng])
                        if first_z[zeng]:
                            w.wait(zeng, "T", prev_zmm[par])
                            w.ins(zeng, lambda e, pb=pb, acc=acc: e.tensor_copy(
                                acc[:, :], self.PT[:, pb, :]), inc=zctr)
                            first_z[zeng] = False
                        else:
                            w.ins(zeng, self._lp(lambda e, pb=pb, acc=acc: e.tensor_tensor(
                                out=acc[:, :], in0=acc[:, :],
                                in1=self.PT[:, pb, :], op=ALU.add)), inc=zctr)
                        z_last[zeng] = w.snap(zctr)
                        pt_z = (zctr, z_last[zeng])
                    if g < c.NG:
                        for _ji in range(len(my_jobs)):
                            if g == min(3 + 6 * _ji, c.NG - 1):
                                gamma_one(my_jobs[_ji][0], my_jobs[_ji][1][0], my_jobs[_ji][1][2])
                    if g == min(1, c.NG - 1):
                        fin_stage1(w)
                    if g == min(8, c.NG):
                        fin_stage2(w)
                    gp = g - 1
                    if gp >= 0:
                        pbp = gp % 6
                        w.wait("tensor", "S", exp_snap[gp])
                        if gp == 0 and qc == 0:
                            w.wait("tensor", vsem, ldv_h)
                        for i in range(c.G):
                            kt = gp * c.G + i
                            w.ins("tensor", lambda e, kt=kt, i=i, hb=hb, ob=ob, pbp=pbp: e.matmul(
                                ob[:, :],
                                self.VH[:, hb, kt, :],
                                self.PT[:, pbp, i * c.QCW:(i + 1) * c.QCW],
                                start=(kt == 0), stop=(kt == c.KT - 1),
                                skip_group_check=True),
                                inc="T" if i == c.G - 1 else None)
                        pt_user[pbp] = (w.snap("T"), pt_z_of[gp][0], pt_z_of[gp][1])
                    if g < c.NG:
                        exp_snap[g] = last_exp
                        pt_z_of[g] = pt_z
                z_last["pv_done"] = w.snap("T")
                pending_fin[0] = {"t": (h, qc, gi_, ob, dict(z_last), par), "stage": 0}
            self._tp_tags[("kh", hb)] = w.snap("T")
        fin_stage1(w)
        fin_stage2(w)

        while job_idx[0] < len(jobs):
            ji = job_idx[0]
            gidx, apf, width = jobs[job_idx[0]]
            job_idx[0] += 1
            slot = ji % 6
            gsem = f"ldg{slot}"
            tag = gslot_tag.get(slot)
            if tag is not None:
                w.wait("gpsimd", tag[0], tag[1])
            w.ins("gpsimd", lambda e, apf=apf, slot=slot, width=width: e.dma_start(
                out=self.gbuf[:, slot, 0:width], in_=apf()), inc=gsem, by=16)
            w.wait("scalar", gsem, (ji // 6 + 1) * 16)
            w.wait("scalar", "S", w.snap("S"))
            w.ins("scalar", lambda e, slot=slot, width=width: e.activation(
                out=self.sqb[:, 0:width], in_=self.gbuf[:, slot, 0:width],
                func=AF.Abs, accum_out=self.gtmp), inc="S")
            gslot_tag[slot] = ("S", w.snap("S"))
            w.wait("scalar", "S", w.snap("S"))
            if gam_first[gidx]:
                w.ins("scalar", lambda e, gidx=gidx: e.copy(self.gacc[gidx], self.gtmp), inc="S")
                gam_first[gidx] = False
            else:
                w.ins("scalar", lambda e, gidx=gidx: e.add(
                    self.gacc[gidx], self.gacc[gidx], self.gtmp), inc="S")
        self.g_done = w.snap("S")

        self.emit_gamma_finalize(w)
        if c.dbg:
            v_all = w.snap("V")
            w.wait("gpsimd", "V", v_all)
            w.ins("gpsimd", lambda e: e.dma_start(
                out=self.dbg_att[:, :].rearrange("p (a b) -> p a b", b=c.T),
                in_=self.QT[:, :, :]), inc="stc", by=16)
            w.wait("gpsimd", "stc", w.snap("stc"))
            for gi in range(3):
                w.ins("gpsimd", lambda e, gi=gi: e.dma_start(
                    out=self.dbg_thr[:, gi:gi + 1], in_=self.thr[gi],
                    allow_slow_non_contiguous=True), inc="stc", by=16)
            w.wait("gpsimd", "stc", w.snap("stc"))

    def _gamma_jobs(self):
        c = self.cfg
        jobs = []
        nw = c.D // 1024
        for mi, mat in enumerate([self.w1, self.w2]):
            for j in range(c.HF // 128):
                for k in range(nw):
                    jobs.append((mi, lambda mat=mat, j=j, k=k: mat[j * 128:(j + 1) * 128, k * 1024:(k + 1) * 1024], 1024))
        hq = c.HF // 8
        assert hq <= 1024
        for j in range(c.D // 128):
            for k in range(8):
                jobs.append((2, lambda j=j, k=k: self.w3[j * 128:(j + 1) * 128, k * hq:(k + 1) * hq], hq))
        return jobs

    def emit_gamma_finalize(self, w):
        c = self.cfg
        n_el = float(c.HF * c.D)
        for gi in range(3):
            w.wait("vector", "S", self.g_done)
            w.wait("vector", "V", w.snap("V"))
            w.ins("vector", lambda e, gi=gi: e.tensor_tensor(
                out=self.gaccV[gi], in0=self.gaccV[gi], in1=self.gacc[gi], op=ALU.add), inc="V")
            w.wait("tensor", "V", w.snap("V"))
            w.ins("tensor", lambda e, gi=gi: e.matmul(
                self.ZB[0][0:1, 0:1], self.ones_f[:, 0:1], self.gaccV[gi],
                start=True, stop=True, skip_group_check=True), inc="T")
            t0 = w.snap("T")
            w.wait("scalar", "T", t0)
            w.wait("scalar", "S", w.snap("S"))
            w.ins("scalar", lambda e, gi=gi: e.mul(self.std[0:1, :], self.ZB[0][0:1, 0:1], 0.5 / n_el), inc="S")
            w.wait("scalar", "S", w.snap("S"))
            w.ins("scalar", lambda e: e.add(self.std[0:1, :], self.std[0:1, :], self.cst[0:1, 1:2]), inc="S")
            s1 = w.snap("S")
            w.wait("tensor", "S", s1)
            w.wait("tensor", "V", w.snap("V"))
            w.ins("tensor", lambda e: e.matmul(
                self.ZB[0][:, 0:1], self.ones_f[0:1, :], self.std[0:1, :],
                start=True, stop=True, skip_group_check=True), inc="T")
            t1 = w.snap("T")
            w.wait("vector", "T", t1)
            w.ins("vector", lambda e, gi=gi: e.tensor_copy(self.thr[gi], self.ZB[0][:, 0:1]), inc="V")
            w.wait("vector", "V", w.snap("V"))
            w.ins("vector", lambda e, gi=gi: e.tensor_scalar(
                out=self.nthr[gi], in0=self.thr[gi],
                scalar1=-1.0, scalar2=None, op0=ALU.mult), inc="V")

    def ph_outproj(self, w):
        c = self.cfg
        wto_ec_ready = []
        for et in range(c.E):
            slot = et % 2
            self.wstrip_prep(w, self.wo, et * 128, slot,
                             dst=lambda half, et=et: self.wTo[:, half * 8:(half + 1) * 8,
                                                             et * 128:(et + 1) * 128])
            if et % 4 == 3:
                wto_ec_ready.append(w.snap("V"))

        ev_tag = 0
        for stt in range(c.NT):
            slot = stt % 2
            tag = self._tp_tags.get(("xstrip", slot))
            if tag is not None:
                w.wait("gpsimd", "V", tag)
            xsem = "ldx0" if slot == 0 else "ldx1"
            w.ins("gpsimd", lambda e, stt=stt, slot=slot: e.dma_start(
                out=self.xstrip[:, slot, :],
                in_=self.x[stt * 128:(stt + 1) * 128, :]), inc=xsem, by=16)
            ld = w.snap(xsem)
            if stt > 0:
                w.wait("tensor", "V", ev_tag)
            for ec in range(c.D // 512):
                if stt == 0:
                    w.wait("tensor", "V", wto_ec_ready[ec])
                for cc_ in range(c.C):
                    w.ins("tensor", lambda e, cc_=cc_, stt=stt, ec=ec: e.matmul(
                        self.poC[:, ec * 512:(ec + 1) * 512],
                        self.QT[:, cc_, stt * 128:(stt + 1) * 128],
                        self.wTo[:, cc_, ec * 512:(ec + 1) * 512],
                        start=(cc_ == 0), stop=(cc_ == c.C - 1), skip_group_check=True),
                        inc="T" if (cc_ == c.C - 1 and ec == c.D // 512 - 1) else None)
            mm = w.snap("T")
            w.wait("vector", "T", mm)
            w.wait("vector", xsem, ld)
            x1tag = self._tp_tags.get("x1f")
            if x1tag is not None:
                w.wait("vector", "V", x1tag)
            w.ins("vector", lambda e, slot=slot: e.tensor_tensor(
                out=self.x1f[:, :], in0=self.xstrip[:, slot, :],
                in1=self.poC[:, :], op=ALU.add), inc="V")
            v_x1 = w.snap("V")
            ev_tag = v_x1
            self._tp_tags[("xstrip", slot)] = v_x1
            w.wait("vector", "V", v_x1)
            w.wait("vector", "stc", w.snap("stc"))
            w.ins("vector", lambda e: e.tensor_copy(self.x1c[:, :], self.x1f[:, :]), inc="V")
            v_x1c = w.snap("V")
            w.wait("gpsimd", "V", v_x1c)
            w.ins("gpsimd", lambda e, stt=stt: e.dma_start(
                out=self.x1_dram[:, stt * c.D:(stt + 1) * c.D], in_=self.x1c[:, :]), inc="stc", by=16)
            w.wait("scalar", "V", v_x1)
            nb_war = ("stc", w.snap("stc")) if stt > 1 else None
            v_ln = self.ln_tile(w, self.x1f[:, :], self.nb[:, slot, :], out_war=nb_war, idx=slot,
                                disc=self.wbf[:, slot, 0:c.D // 2])
            self._tp_tags["x1f"] = v_ln
            w.wait("gpsimd", "V", v_ln)
            w.ins("gpsimd", lambda e, stt=stt, slot=slot: e.dma_start(
                out=self.xn2_dram[:, stt * c.D:(stt + 1) * c.D], in_=self.nb[:, slot, :]), inc="stc", by=16)

    def ph_ffn1(self, w):
        c = self.cfg
        xn2_sc_ready = []
        for stt in range(c.NT):
            slot = stt % 2
            ntag = self._tp_tags.get(("nb", slot))
            if ntag is not None:
                w.wait("sync", "T", ntag)
            hsem = "ldh0" if slot == 0 else "ldh1"
            w.ins("sync", lambda e, stt=stt, slot=slot: e.dma_start(
                out=self.nb[:, slot, :], in_=self.xn2_dram[:, stt * c.D:(stt + 1) * c.D]), inc=hsem, by=16)
            ld = w.snap(hsem)
            w.wait("tensor", hsem, ld)
            t_tp = self.transpose_tile_to(w, self.nb[:, slot, :], self.xn2T, stt, wait_v=None)
            self._tp_tags[("nb", slot)] = t_tp
            if (stt + 1) % (c.SCW // 128) == 0:
                xn2_sc_ready.append(w.snap("V"))

        w3_q_at = {}
        step = max(1, c.HT // c.E)
        for k in range(c.E):
            w3_q_at.setdefault(min(k * step, c.HT - 2), []).append(k)
        w3_pending = []

        def prep_q(ht):
            for im, (mat, gi) in enumerate([(self.w1, 0), (self.w2, 1)]):
                slot = im
                tag = self._tp_tags.get(("wstrip2", slot))
                if tag is not None:
                    w.wait("sync", "V", tag)
                lsem = "ldw0" if slot == 0 else "ldw1"
                w.ins("sync", lambda e, mat=mat, ht=ht, slot=slot: e.dma_start(
                    out=self.wstrip2[:, slot, :], in_=mat[ht * 128:(ht + 1) * 128, :]), inc=lsem, by=16)
                ld = w.snap(lsem)
                w.wait("vector", lsem, ld)
                w.wait("vector", "V", w.snap("V"))
                qtag = self._tp_tags.get(("qbf", slot))
                if qtag is not None:
                    w.wait("vector", "T", qtag)
                w.ins("vector", lambda e, slot=slot, gi=gi: e.tensor_scalar(
                    out=self.nb[:, 0, :], in0=self.wstrip2[:, slot, :],
                    scalar1=self.thr[gi], scalar2=None, op0=ALU.is_ge), inc="V")
                w.ins("vector", lambda e, slot=slot, gi=gi: e.tensor_scalar(
                    out=self.sgnb[:, :], in0=self.wstrip2[:, slot, :],
                    scalar1=self.nthr[gi], scalar2=None, op0=ALU.is_ge), inc="V")
                vb = w.snap("V")
                w.wait("vector", "V", vb)
                w.ins("vector", lambda e, slot=slot: e.scalar_tensor_tensor(
                    out=self.qbf[:, slot, :], in0=self.nb[:, 0, :], scalar=-1.0,
                    in1=self.sgnb[:, :], op0=ALU.add, op1=ALU.add), inc="V")
                self._tp_tags[("wstrip2", slot)] = w.snap("V")
            return w.snap("V")

        def prep_t(ht, q_ready):
            bslot = ht % 2
            for im, qT in enumerate([self.q1T, self.q2T]):
                slot = im
                for half in range(c.C // 8):
                    bank = self.tb[half % 2]
                    btag = self._tp_tags.get(("tb", half % 2))
                    if btag is not None:
                        w.wait("tensor", "V", btag)
                    if half == 0:
                        w.wait("tensor", "V", q_ready)
                        utag = self._tp_tags.get(("qTuse", bslot))
                        if utag is not None:
                            w.wait("tensor", "T", utag)
                    for j in range(8):
                        cc_ = half * 8 + j
                        w.ins("tensor", lambda e, cc_=cc_, j=j, bank=bank, slot=slot: e.transpose(
                            bank[:, j * 128:(j + 1) * 128],
                            self.qbf[:, slot, cc_ * 128:(cc_ + 1) * 128],
                            self.id_b[:, :]),
                            inc="T" if j == 7 else None)
                    td = w.snap("T")
                    self._tp_tags[("qbf", slot)] = td
                    w.wait("vector", "T", td)
                    w.ins("vector", lambda e, half=half, bank=bank, qT=qT, bslot=bslot: e.tensor_copy(
                        qT[:, bslot, half * 8:(half + 1) * 8, :],
                        bank[:, :].rearrange("p (j f) -> p j f", f=128)), inc="V")
                    self._tp_tags[("tb", half % 2)] = w.snap("V")
            return [("V", w.snap("V"))]

        qv = prep_q(0)
        prep_ready = {0: prep_t(0, qv)}
        for ht in range(c.HT):
            bslot = ht % 2
            if ht + 1 < c.HT:
                qv = prep_q(ht + 1)
            for sc in range(c.NSC):
                p1 = self.pfD[(2 * sc) % 4]
                p2 = self.pfD[(2 * sc + 1) % 4]
                pt1 = self._tp_tags.get(("pf", (2 * sc) % 4))
                if pt1 is not None:
                    w.wait("tensor", "V", pt1)
                pt2 = self._tp_tags.get(("pf", (2 * sc + 1) % 4))
                if pt2 is not None:
                    w.wait("tensor", "V", pt2)
                if sc == 0:
                    for tsem, tcnt in prep_ready[ht]:
                        w.wait("tensor", tsem, tcnt)
                if ht == 0:
                    w.wait("tensor", "V", xn2_sc_ready[sc])
                for cc_ in range(c.C):
                    w.ins("tensor", lambda e, cc_=cc_, sc=sc, p1=p1, bslot=bslot: e.matmul(
                        p1[:, 0:c.SCW], self.q1T[:, bslot, cc_, :],
                        self.xn2T[:, cc_, sc * c.SCW:(sc + 1) * c.SCW],
                        start=(cc_ == 0), stop=(cc_ == c.C - 1)),
                        inc="T" if cc_ == c.C - 1 else None)
                mm1 = w.snap("T")
                for cc_ in range(c.C):
                    w.ins("tensor", lambda e, cc_=cc_, sc=sc, p2=p2, bslot=bslot: e.matmul(
                        p2[:, 0:c.SCW], self.q2T[:, bslot, cc_, :],
                        self.xn2T[:, cc_, sc * c.SCW:(sc + 1) * c.SCW],
                        start=(cc_ == 0), stop=(cc_ == c.C - 1)),
                        inc="T" if cc_ == c.C - 1 else None)
                mm2 = w.snap("T")
                self._tp_tags[("qTuse", bslot)] = mm2
                w.wait("scalar", "T", mm1)
                stag = self._tp_tags.get(("sig", sc % 2))
                if stag is not None:
                    w.wait("scalar", "V", stag)
                w.ins("scalar", lambda e, sc=sc, p1=p1: e.activation(
                    out=self.sig[:, sc % 2, 0:c.SCW], in_=p1[:, 0:c.SCW], func=AF.Sigmoid), inc="S")
                s_sig = w.snap("S")
                w.wait("vector", "S", s_sig)
                w.wait("vector", "T", mm2)
                htag = self._tp_tags.get(("hidb", sc % 2))
                if htag is not None:
                    w.wait("vector", "sth0" if sc % 2 == 0 else "sth1", htag)
                w.ins("vector", lambda e, sc=sc, p2=p2: e.tensor_tensor(
                    out=self.hidb[:, sc % 2, 0:c.SCW], in0=p2[:, 0:c.SCW],
                    in1=self.sig[:, sc % 2, 0:c.SCW], op=ALU.mult), inc="V")
                v_hid = w.snap("V")
                self._tp_tags[("sig", sc % 2)] = v_hid
                self._tp_tags[("pf", (2 * sc) % 4)] = v_hid
                self._tp_tags[("pf", (2 * sc + 1) % 4)] = v_hid
                w.wait("gpsimd", "V", v_hid)
                w.ins("gpsimd", lambda e, sc=sc, ht=ht: e.dma_start(
                    out=self.hid_dram[sc * 128:(sc + 1) * 128, ht * c.SCW:(ht + 1) * c.SCW],
                    in_=self.hidb[:, sc % 2, 0:c.SCW]), inc="sth0" if sc % 2 == 0 else "sth1", by=16)
                self._tp_tags[("hidb", sc % 2)] = w.snap("sth0" if sc % 2 == 0 else "sth1")

            if ht + 1 < c.HT:
                prep_ready[ht + 1] = prep_t(ht + 1, qv)
            if w3_pending:
                k, qr = w3_pending.pop()
                self.emit_w3_tp(w, k, qr)
            due = w3_q_at.get(ht, [])
            for k in due[:-1]:
                self.emit_w3_tp(w, k, self.emit_w3_quant(w, k))
            if due:
                w3_pending.append((due[-1], self.emit_w3_quant(w, due[-1])))
        while w3_pending:
            k, qr = w3_pending.pop()
            self.emit_w3_tp(w, k, qr)

    def emit_w3_quant(self, w, et):
        c = self.cfg
        tag = self._tp_tags.get("w3f")
        if tag is not None:
            w.wait("sync", "V", tag)
        w.ins("sync", lambda e, et=et: e.dma_start(
            out=self.w3f[:, :], in_=self.w3[et * 128:(et + 1) * 128, :]), inc="ldw3", by=16)
        ld = w.snap("ldw3")
        w.wait("vector", "ldw3", ld)
        gtag = self._tp_tags.get("geb")
        if gtag is not None:
            w.wait("vector", "V", gtag)
        w.ins("vector", lambda e: e.tensor_scalar(
            out=self.geb[:, :], in0=self.w3f[:, :],
            scalar1=self.thr[2], scalar2=None, op0=ALU.is_ge), inc="V")
        q3tag = self._tp_tags.get("q3s")
        if q3tag is not None:
            w.wait("vector", "T", q3tag)
        w.ins("vector", lambda e: e.tensor_scalar(
            out=self.q3s[:, :], in0=self.w3f[:, :],
            scalar1=self.nthr[2], scalar2=None, op0=ALU.is_ge), inc="V")
        vb = w.snap("V")
        self._tp_tags["w3f"] = vb
        w.wait("vector", "V", vb)
        w.ins("vector", lambda e: e.scalar_tensor_tensor(
            out=self.q3s[:, :], in0=self.geb[:, :], scalar=-1.0,
            in1=self.q3s[:, :], op0=ALU.add, op1=ALU.add), inc="V")
        self._tp_tags["geb"] = w.snap("V")
        return w.snap("V")

    def emit_w3_tp(self, w, et, q_ready):
        c = self.cfg
        stg_tag = self._tp_tags.get("q3stage")
        for bi in range(c.HT):
            bank = self.tb[(bi // 8) % 2]
            j = bi % 8
            if j == 0:
                btag = self._tp_tags.get(("tb", (bi // 8) % 2))
                if btag is not None:
                    w.wait("tensor", "V", btag)
                if bi == 0:
                    w.wait("tensor", "V", q_ready)
                    if stg_tag is not None:
                        w.wait("tensor", "stc", w.snap("stc"))
            last = (j == 7) or (bi == c.HT - 1)
            w.ins("tensor", lambda e, bi=bi, j=j, bank=bank: e.transpose(
                bank[:, j * 128:(j + 1) * 128],
                self.q3s[:, bi * 128:(bi + 1) * 128],
                self.id_b[:, :]),
                inc="T" if last else None)
            if last:
                td = w.snap("T")
                self._tp_tags["q3s"] = td
                w.wait("vector", "T", td)
                lo = (bi // 8) * 8
                hi = bi + 1
                w.ins("vector", lambda e, lo=lo, hi=hi, bank=bank: e.tensor_copy(
                    self.q3stage[:, lo:hi, :],
                    bank[:, 0:(hi - lo) * 128].rearrange("p (j f) -> p j f", f=128)), inc="V")
                self._tp_tags[("tb", (bi // 8) % 2)] = w.snap("V")
        v_stage = w.snap("V")
        w.wait("gpsimd", "V", v_stage)
        w.ins("gpsimd", lambda e, et=et: e.dma_start(
            out=self.q3t_dram[et * 128:(et + 1) * 128, :].rearrange("p (t f) -> p t f", f=128),
            in_=self.q3stage[:, :, :]), inc="stc", by=16)
        self._tp_tags["q3stage"] = w.snap("stc")

    def ph_ffn2(self, w):
        c = self.cfg
        NTL = c.SCW // 128

        def load_hid(sc):
            slot = sc % 2
            hsem = "ldh0" if slot == 0 else "ldh1"
            htag = self._tp_tags.get(("hid", slot))
            if htag is not None:
                w.wait("sync", "T", htag)
            w.ins("sync", lambda e, sc=sc, slot=slot: e.dma_start(
                out=self.hid[:, slot, :, :],
                in_=self.hid_dram[sc * 128:(sc + 1) * 128, :]
                    .rearrange("p (t f) -> p t f", f=c.SCW)), inc=hsem, by=16)
            return hsem, w.snap(hsem)

        hid_ready = {0: load_hid(0)}
        for sc in range(c.NSC):
            hslot = sc % 2
            ld_hid = hid_ready[sc]
            otag = self._tp_tags.get("outsb")
            if otag is not None:
                w.wait("gpsimd", "stc", w.snap("stc"))
            w.ins("gpsimd", lambda e, sc=sc: e.dma_start(
                out=self.outsb[:, :, :],
                in_=self.x1_dram[:, sc * NTL * c.D:(sc + 1) * NTL * c.D]
                    .rearrange("p (t f) -> p t f", f=c.D)), inc="ldg", by=16)
            ld_x1 = w.snap("ldg")
            for et in range(c.E):
                qslot = et % 2
                qtag = self._tp_tags.get(("q3rd", qslot))
                if qtag is not None:
                    w.wait("sync", "T", qtag)
                qsem = "ldq0" if qslot == 0 else "ldq1"
                w.ins("sync", lambda e, et=et, qslot=qslot: e.dma_start(
                    out=self.q3rd[:, qslot, :, :],
                    in_=self.q3t_dram[et * 128:(et + 1) * 128, :]
                        .rearrange("p (t f) -> p t f", f=128)), inc=qsem, by=16)
                ld_q3 = w.snap(qsem)
                ab = self.accE[et % 2]
                atag = self._tp_tags.get(("acc", et % 2))
                if atag is not None:
                    w.wait("tensor", "V", atag)
                w.wait("tensor", qsem, ld_q3)
                if et == 0:
                    w.wait("tensor", ld_hid[0], ld_hid[1])
                for ht in range(c.HT):
                    w.ins("tensor", lambda e, ht=ht, qslot=qslot, ab=ab, hslot=hslot: e.matmul(
                        ab[:, 0:c.SCW], self.q3rd[:, qslot, ht, :],
                        self.hid[:, hslot, ht, :],
                        start=(ht == 0), stop=(ht == c.HT - 1)),
                        inc="T" if ht == c.HT - 1 else None)
                mm = w.snap("T")
                self._tp_tags[("q3rd", qslot)] = mm
                self._tp_tags[("hid", hslot)] = mm
                if et == 0 and sc + 1 < c.NSC:
                    hid_ready[sc + 1] = load_hid(sc + 1)
                w.wait("vector", "T", mm)
                pan_tag = self._tp_tags.get(("panel", et % 2))
                if pan_tag is not None:
                    w.wait("vector", "T", pan_tag)
                w.ins("vector", lambda e, et=et, ab=ab: e.tensor_copy(
                    self.panel[:, et % 2, 0:c.SCW], ab[:, 0:c.SCW]), inc="V")
                v_pan = w.snap("V")
                self._tp_tags[("acc", et % 2)] = v_pan
                tz = self.tpzE[et % 2]
                tztag = self._tp_tags.get(("tpz", et % 2))
                if tztag is not None:
                    w.wait("tensor", "V", tztag)
                w.wait("tensor", "V", v_pan)
                for j in range(NTL):
                    w.ins("tensor", lambda e, j=j, et=et, tz=tz: e.transpose(
                        tz[:, j * 128:(j + 1) * 128],
                        self.panel[:, et % 2, j * 128:(j + 1) * 128],
                        self.id_b[:, :]),
                        inc="T" if j == NTL - 1 else None)
                t_tz = w.snap("T")
                self._tp_tags[("panel", et % 2)] = t_tz
                w.wait("vector", "T", t_tz)
                if et == 0:
                    w.wait("vector", "ldg", ld_x1)
                for j in range(NTL):
                    w.ins("vector", lambda e, j=j, et=et, tz=tz: e.tensor_tensor(
                        out=self.outsb[:, j, et * 128:(et + 1) * 128],
                        in0=self.outsb[:, j, et * 128:(et + 1) * 128],
                        in1=tz[:, j * 128:(j + 1) * 128], op=ALU.add), inc="V")
                self._tp_tags[("tpz", et % 2)] = w.snap("V")
            v_out = w.snap("V")
            w.wait("gpsimd", "V", v_out)
            for j in range(NTL):
                w.ins("gpsimd", lambda e, j=j, sc=sc: e.dma_start(
                    out=self.out[(sc * NTL + j) * 128:(sc * NTL + j + 1) * 128, :],
                    in_=self.outsb[:, j, :]), inc="stc", by=16)
            self._tp_tags["outsb"] = w.snap("stc")
        w.wait("gpsimd", "stc", w.snap("stc"))


def build_nc(cfg):
    b = Builder(cfg)
    nc = b.nc
    b.alloc_rest()

    sems = {}
    sem_names = ["bar", "ldpro", "ldx0", "ldx1", "ldw0", "ldw1", "ldw3", "ldh", "ldh0", "ldh1",
                 "ldq0", "ldq1", "ldk0", "ldk1", "ldv0", "ldv1", "ldg", "ldg0", "ldg1", "ldg2", "ldg3", "ldg4", "ldg5", "sth0", "sth1", "sq0", "sq1", "tq0", "tq1", "se0", "se1",
                 "stc", "ccK", "ccV", "V", "T", "S", "G"]
    sem_stack = contextlib.ExitStack()
    for name in sem_names:
        sems[name] = sem_stack.enter_context(nc.semaphore(name))

    with nc.Block() as block:

        @block.sync
        def _(e):
            b.emit(Walk("sync", e, sems))

        @block.vector
        def _(e):
            b.emit(Walk("vector", e, sems))

        @block.tensor
        def _(e):
            b.emit(Walk("tensor", e, sems))

        @block.scalar
        def _(e):
            b.emit(Walk("scalar", e, sems))

        @block.gpsimd
        def _(e):
            b.emit(Walk("gpsimd", e, sems))

    b.close_all()
    sem_stack.close()
    return nc


def make_inputs(cfg, x, wq, wk, wv, wo, w1, w2, w3):
    c = cfg
    xf = np.ascontiguousarray(np.asarray(x, dtype=np.float32).reshape(c.B * c.S, c.D))
    ident = np.eye(128, dtype=np.float16)
    ones = np.ones((128, 128), dtype=np.float32)
    consts = np.tile(np.array([[1e-5, 0.5e-8]], dtype=np.float32), (128, 1))
    common = dict(
        wq=np.ascontiguousarray(np.asarray(wq, np.float32)),
        wk=np.ascontiguousarray(np.asarray(wk, np.float32)),
        wv=np.ascontiguousarray(np.asarray(wv, np.float32)),
        wo=np.ascontiguousarray(np.asarray(wo, np.float32)),
        w1=np.ascontiguousarray(np.asarray(w1, np.float32)),
        w2=np.ascontiguousarray(np.asarray(w2, np.float32)),
        w3=np.ascontiguousarray(np.asarray(w3, np.float32)),
        ident=ident, ones=ones, consts=consts,
    )
    return [dict(common, x=np.ascontiguousarray(xf[i * c.T:(i + 1) * c.T]))
            for i in range(c.ncores)]


def kernel(L0_attn_x, L0_attn_w_q, L0_attn_w_k, L0_attn_w_v, L0_attn_w_o,
           L0_ffn_w1, L0_ffn_w2, L0_ffn_w3):
    cfg = Cfg()
    nc = build_nc(cfg)
    in_maps = make_inputs(cfg, L0_attn_x, L0_attn_w_q, L0_attn_w_k, L0_attn_w_v,
                          L0_attn_w_o, L0_ffn_w1, L0_ffn_w2, L0_ffn_w3)
    res = run_bass_kernel_spmd(nc, in_maps, core_ids=list(range(cfg.ncores)))
    shards = [np.asarray(res.results[i]["out"], dtype=np.float32).reshape(cfg.T, cfg.D)
              for i in range(cfg.ncores)]
    return np.concatenate(shards, axis=0).reshape(cfg.B, cfg.S, cfg.D)
